# version 19
# baseline (speedup 1.0000x reference)
import math
import numpy as np
import ml_dtypes
from contextlib import ExitStack

import concourse.bass as bass
import concourse.mybir as mybir
from concourse.bass_utils import run_bass_kernel_spmd

F32 = mybir.dt.float32
BF16 = mybir.dt.bfloat16
AF = mybir.ActivationFunctionType
ALU = mybir.AluOpType
AX = mybir.AxisListType
bf = ml_dtypes.bfloat16

P = 128
DMODEL = 2048
NCH = 16
TOKW = 1152
NTOK = 1024
HALO = 128
NKV = 8
LN_EPS = 1e-5
RMS_EPS = 1.1920929e-07
W1_SLABS = 16
W2_SLABS = 4
DEBUG = False
PHASES = 99
PH3_LEVEL = 9


def _rope_cos_sin(t_len):
    HEAD_DIM = 64
    ROPE_THETA = 150000.0
    SCALING = 32.0
    NTK_ALPHA = 1.0
    NTK_BETA = 32.0
    ICL = 1024
    d_half = HEAD_DIM // 2
    freq = ROPE_THETA ** (np.arange(0, HEAD_DIM, 2, dtype=np.float32) / HEAD_DIM)
    concentration = 0.1 * math.log(SCALING) + 1.0
    low = d_half * math.log(ICL / (NTK_BETA * 2 * math.pi)) / math.log(ROPE_THETA)
    high = d_half * math.log(ICL / (NTK_ALPHA * 2 * math.pi)) / math.log(ROPE_THETA)
    interpolation = 1.0 / (SCALING * freq)
    extrapolation = 1.0 / freq
    ramp = np.clip((np.arange(d_half, dtype=np.float32) - low) / (high - low), 0.0, 1.0)
    mask = 1.0 - ramp
    inv_freq = interpolation * (1.0 - mask) + extrapolation * mask
    freqs = np.outer(np.arange(t_len, dtype=np.float32), inv_freq)
    return np.cos(freqs) * concentration, np.sin(freqs) * concentration


def _host_prep(x, qkv_w, qkv_b, out_w, out_b, sinks, W_fc, W_proj):
    f32 = np.float32
    sh = {}
    sh["qkv_wT"] = np.ascontiguousarray(qkv_w.T.astype(f32, copy=False))
    sh["qkv_bT"] = np.ascontiguousarray(
        qkv_b.astype(f32, copy=False)[:2560].reshape(20, P).T)
    sh["vbias"] = np.ascontiguousarray(
        qkv_b.astype(f32, copy=False)[2560:].reshape(1, 512))
    sh["out_wT"] = np.ascontiguousarray(out_w.T.astype(f32, copy=False))
    sh["out_bC"] = np.ascontiguousarray(
        out_b.astype(f32, copy=False).reshape(16, P).T)
    sh["sinks128"] = np.ascontiguousarray(
        np.broadcast_to(sinks.astype(f32, copy=False), (P, 32)))
    sh["w1T"] = np.ascontiguousarray(W_fc.T.astype(f32, copy=False))
    sh["w2T"] = np.ascontiguousarray(W_proj.T.astype(f32, copy=False))
    sh["ident"] = np.eye(P, dtype=bf)
    sh["ones_col"] = np.ones((P, 1), f32)
    sh["ones_row"] = np.ones((1, P), f32)
    sperm = np.zeros((P, P), f32)
    for m in range(P):
        if m % 64 < 32:
            sperm[m + 32, m] = -1.0
        else:
            sperm[m - 32, m] = 1.0
    sh["sperm"] = sperm
    r = np.arange(P)[:, None]
    c = np.arange(P)[None, :]
    maskP = (c > r).astype(bf)
    maskD = (c <= r).astype(bf)
    sh["maskPD"] = np.ascontiguousarray(np.concatenate([maskP, maskD], axis=1))
    maskPD0 = sh["maskPD"].copy()
    maskPD0[:, :P] = 0
    gp = np.zeros((P, 16 * 32), dtype=bf)
    for cc in range(16):
        for p in range(P):
            gp[p, 32 * cc + 2 * cc + p // 64] = 1.0
    sh["gp"] = gp
    ep = np.zeros((32, 16 * P), f32)
    for cc in range(16):
        for p in range(P):
            ep[2 * cc + p // 64, P * cc + p] = 1.0
    sh["ep"] = ep
    sh["epb"] = ep.astype(bf)
    sh["epb2"] = np.concatenate([ep, ep], axis=0).astype(bf)

    cos_t, sin_t = _rope_cos_sin(4096)
    cores = []
    for cid in range(8):
        b, j = cid // 4, cid % 4
        t0 = NTOK * j
        d = {}
        xb = x[b].astype(f32, copy=False)
        xc = np.zeros((TOKW, DMODEL), f32)
        lo = max(0, t0 - HALO)
        xc[HALO - (t0 - lo):] = xb[lo:t0 + NTOK]
        d["x_fm"] = np.ascontiguousarray(xc.T)
        pos = np.arange(t0 - HALO, t0 + NTOK)
        posc = np.clip(pos, 0, 4095)
        ct = cos_t[posc].copy()
        st_ = sin_t[posc].copy()
        ct[pos < 0] = 0.0
        st_[pos < 0] = 0.0
        idx = np.arange(P) % 32
        d["cos128"] = np.ascontiguousarray(ct[:, idx].T)
        d["sin128"] = np.ascontiguousarray(st_[:, idx].T)
        d["qcos128"] = np.ascontiguousarray(ct[128:, idx].T * np.float32(0.125))
        d["qsin128"] = np.ascontiguousarray(st_[128:, idx].T * np.float32(0.125))
        d["maskPD0"] = maskPD0 if j == 0 else sh["maskPD"]
        cores.append(d)
    return sh, cores


class _Res:
    __slots__ = ("name", "w", "readers", "strict")

    def __init__(self, name):
        self.name = name
        self.w = None
        self.readers = {}
        self.strict = False


class Sched:
    ENGS = ("sync", "vector", "tensor", "scalar", "gpsimd")

    def __init__(self, nc, S):
        self.nc = nc
        self.prog = {e: [] for e in self.ENGS}
        self.trk = {e: [S(nc.semaphore("m_" + e)), 0] for e in self.ENGS}
        self.dma = {}
        self.S = S
        self.res = {}
        self.wm = {}
        self.floor = {}
        self.nsteps = 0
        self.nwaits = 0
        self.nfence = 0

    def r(self, name):
        if name not in self.res:
            self.res[name] = _Res(name)
        return self.res[name]

    def _sem(self, semname):
        if semname.startswith("d:"):
            return self.dma[semname[2:]][0]
        return self.trk[semname][0]

    def step(self, eng, reads, writes, fn, dma=None):
        reads = [self.r(n) for n in reads]
        writes = [self.r(n) for n in writes]
        waits = {}

        def need(dep, strict=False):
            if dep is None:
                return
            deng, dsem, dval = dep
            if deng == eng and not dsem.startswith("d:") and not strict:
                return
            key = (eng, dsem)
            if self.wm.get(key, -1) >= dval:
                return
            if waits.get(dsem, -1) < dval:
                waits[dsem] = dval

        for fsem, fval in self.floor.items():
            need(("_fence", fsem, fval))
        for rr in reads:
            need(rr.w, rr.strict)
        for ww in writes:
            need(ww.w, ww.strict)
            for dep in ww.readers.values():
                need(dep, ww.strict)
        if dma is not None:
            if dma not in self.dma:
                self.dma[dma] = [self.S(self.nc.semaphore("d_" + dma)), 0]
            t = self.dma[dma]
            t[1] += 16
            semname, val, inc = "d:" + dma, t[1], 16
        else:
            t = self.trk[eng]
            t[1] += 1
            semname, val, inc = eng, t[1], 1
        for ww in writes:
            ww.w = (eng, semname, val)
            ww.readers = {}
        for rr in reads:
            rr.readers[eng] = (eng, semname, val)
        for dsem, dval in waits.items():
            self.wm[(eng, dsem)] = dval
        wl = sorted(waits.items())
        self.nsteps += 1
        self.nwaits += len(wl)
        sem_h = self._sem(semname)

        def go(engobj, wl=wl, fn=fn, sem_h=sem_h, inc=inc):
            for dsem, dval in wl:
                engobj.wait_ge(self._sem(dsem), dval)
            fn(engobj).then_inc(sem_h, inc)

        self.prog[eng].append(go)

    def strict(self, *names):
        for n in names:
            self.r(n).strict = True

    def fence(self):
        self.nfence += 1
        allres = list(self.res.keys())
        marks = {}
        for e in self.ENGS:
            self.step(e, allres, [f"_f{self.nfence}_{e}"],
                      lambda x: x.nop())
            marks[e] = self.trk[e][1]
        self.floor = dict(marks)

    def emit(self):
        with self.nc.Block() as blk:
            @blk.sync
            def _(e):
                for g in self.prog["sync"]:
                    g(e)

            @blk.vector
            def _(e):
                for g in self.prog["vector"]:
                    g(e)

            @blk.tensor
            def _(e):
                for g in self.prog["tensor"]:
                    g(e)

            @blk.scalar
            def _(e):
                for g in self.prog["scalar"]:
                    g(e)

            @blk.gpsimd
            def _(e):
                for g in self.prog["gpsimd"]:
                    g(e)


def build_nc():
    nc = bass.Bass("TRN2", target_bir_lowering=False, debug=False,
                   detect_race_conditions=False)

    def inp(name, shape, dt=F32):
        return nc.dram_tensor(name, shape, dt, kind="ExternalInput").ap()

    x_fm = inp("x_fm", [DMODEL, TOKW])
    qkv_wT_d = inp("qkv_wT", [DMODEL, 3072])
    qkv_bT_d = inp("qkv_bT", [P, 20])
    vbias_d = inp("vbias", [1, 512])
    out_wT_d = inp("out_wT", [DMODEL, DMODEL])
    out_bC_d = inp("out_bC", [P, 16])
    sinks_d = inp("sinks128", [P, 32])
    w1T_d = inp("w1T", [DMODEL, 8192])
    w2T_d = inp("w2T", [8192, DMODEL])
    cos_d = inp("cos128", [P, TOKW])
    sin_d = inp("sin128", [P, TOKW])
    qcos_d = inp("qcos128", [P, NTOK])
    qsin_d = inp("qsin128", [P, NTOK])
    ident_d = inp("ident", [P, P], BF16)
    onesc_d = inp("ones_col", [P, 1])
    onesr_d = inp("ones_row", [1, P])
    sperm_d = inp("sperm", [P, P])
    maskPD_d = inp("maskPD", [P, 256], BF16)
    maskPD0_d = inp("maskPD0", [P, 256], BF16)
    gp_d = inp("gp", [P, 512], BF16)
    ep_d = inp("ep", [32, 2048])
    epb_d = inp("epb", [32, 2048], BF16)
    epb2_d = inp("epb2", [64, 2048], BF16)

    o_fm = nc.dram_tensor("o_fm", [DMODEL, NTOK], F32, kind="ExternalOutput").ap()
    dbg = {}
    if DEBUG:
        for nm, shp, dt in [("xn", [P, NCH * TOKW], BF16),
                            ("kb", [P, NKV * TOKW], BF16),
                            ("vt", [P, 9 * 512], BF16),
                            ("attn0", [P, NCH * 512], BF16),
                            ("attn1", [P, NCH * 512], BF16),
                            ("acc1", [P, NCH * NTOK], F32),
                            ("acc2", [P, NCH * NTOK], F32),
                            ("rstd", [P, NTOK], F32),
                            ("sc0", [P, 512], F32), ("sc1", [P, 512], F32),
                            ("smal", [P, 20], F32),
                            ("praw", [P, 1024], BF16),
                            ("ptsb", [P, 1024], BF16)]:
            dbg[nm] = nc.dram_tensor("dbg_" + nm, shp, dt,
                                     kind="ExternalOutput").ap()
    u_scr = nc.dram_tensor("u_scr", [8192, NTOK], BF16, kind="Internal").ap()
    w2q_scr = nc.dram_tensor("w2q_scr", [8192, DMODEL], BF16, kind="Internal").ap()

    root = ExitStack()
    S = root.enter_context
    sch = Sched(nc, S)
    st = sch.step
    sch.strict("ms_acc", "scal_t", "rs_t",
               "rs_hl", "rs_lo", "mean", "work", "mrs_r", "rstd_r")
    for _pp in range(2):
        sch.strict(f"rowsum{_pp}", f"rden{_pp}", f"den_t{_pp}",
                   f"m_neg{_pp}", f"es_t{_pp}")

    xn = S(nc.sbuf_tensor("xn", [P, NCH * TOKW], BF16))
    acc = S(nc.sbuf_tensor("acc", [P, NCH * NTOK], F32))
    identt = S(nc.sbuf_tensor("identt", [P, P], BF16))
    ones_col = S(nc.sbuf_tensor("ones_colt", [P, 1], F32))
    ones_row = S(nc.sbuf_tensor("ones_rowt", [1, P], F32))
    spermt = S(nc.sbuf_tensor("spermt", [P, P], F32))
    maskPD = S(nc.sbuf_tensor("maskPDt", [P, 256], BF16))
    maskPD0 = S(nc.sbuf_tensor("maskPD0t", [P, 256], BF16))
    qkvbT = S(nc.sbuf_tensor("qkvbT", [P, 20], F32))
    outbC = S(nc.sbuf_tensor("outbC", [P, 16], F32))
    vbias_t = S(nc.sbuf_tensor("vbias_t", [1, 512], F32))
    sinks_t = S(nc.sbuf_tensor("sinks_t", [P, 32], F32))
    epsln = S(nc.sbuf_tensor("epsln", [1, 1], F32))
    epsrms = S(nc.sbuf_tensor("epsrms", [1, 1], F32))
    rstd128 = S(nc.sbuf_tensor("rstd128", [P, NTOK], F32))

    NSL = ((0, 512), (512, 1024), (1024, TOKW))
    xnr = [f"xn{c}" for c in range(NCH)]

    for dst, src, nm in [(identt, ident_d, "identt"), (ones_col, onesc_d, "ones_col"),
                         (ones_row, onesr_d, "ones_row"), (spermt, sperm_d, "spermt"),
                         (maskPD, maskPD_d, "maskPD"), (maskPD0, maskPD0_d, "maskPD0"),
                         (qkvbT, qkv_bT_d, "qkvbT"), (outbC, out_bC_d, "outbC"),
                         (vbias_t, vbias_d, "vbias"), (sinks_t, sinks_d, "sinks")]:
        st("sync", [], [nm],
           (lambda e, dst=dst, src=src: e.dma_start(out=dst[:], in_=src)),
           dma="tab_" + nm)
    onesb = S(nc.sbuf_tensor("onesb", [P, 1], BF16))
    st("vector", [], ["onesb"], lambda e: e.memset(onesb[:], 1.0))
    st("vector", [], ["epsln", "epsrms"],
       lambda e: (e.memset(epsln[:], LN_EPS), e.memset(epsrms[:], RMS_EPS))[-1])

    ph1 = ExitStack()
    Sp = ph1.enter_context
    xst = [Sp(nc.sbuf_tensor(f"xst{i}", [P, TOKW], F32)) for i in range(3)]
    xsq = Sp(nc.sbuf_tensor("xsq", [P, TOKW], F32))
    statb = Sp(nc.sbuf_tensor("statb", [1, 3 * TOKW], F32))
    mean = statb[:, 0:TOKW]
    work = statb[:, TOKW:2 * TOKW]
    mrs_r = statb[:, 2 * TOKW:3 * TOKW]
    m2 = var = lnv = rs_r = work

    with ExitStack() as ph1a:
        ps_sum = ph1a.enter_context(nc.psum_tensor("ps_sum", [1, TOKW], F32))
        ps_sq = ph1a.enter_context(nc.psum_tensor("ps_sq", [1, TOKW], F32))
        for c in range(NCH):
            bn = f"xst{c % 3}"
            st("sync", [], [bn],
               (lambda e, c=c: e.dma_start(out=xst[c % 3][:],
                                           in_=x_fm[c * P:(c + 1) * P, :])),
               dma=bn)
            st("scalar", [bn], ["xsq"],
               (lambda e, c=c: e.activation(out=xsq[:], in_=xst[c % 3][:],
                                            func=AF.Square)))

            def mm_sums(e, c=c):
                for (lo, hi) in NSL:
                    e.matmul(out=ps_sum[:, lo:hi], lhsT=ones_col[:],
                             rhs=xst[c % 3][:, lo:hi], start=(c == 0),
                             stop=(c == NCH - 1))
                    i = e.matmul(out=ps_sq[:, lo:hi], lhsT=ones_col[:],
                                 rhs=xsq[:, lo:hi], start=(c == 0),
                                 stop=(c == NCH - 1))
                return i
            st("tensor", [bn, "xsq", "ones_col"], ["ps_sum", "ps_sq"], mm_sums)
            st("vector", [bn], [f"acc{c}"],
               (lambda e, c=c: e.tensor_copy(out=acc[:, c * NTOK:(c + 1) * NTOK],
                                             in_=xst[c % 3][:, HALO:])))
        st("vector", ["ps_sum", "ps_sq"], ["mean", "work", "mrs_r"],
           lambda e: (e.tensor_scalar(out=mean, in0=ps_sum[:], scalar1=1.0 / DMODEL,
                                      scalar2=None, op0=ALU.mult),
                      e.tensor_scalar(out=work, in0=ps_sq[:], scalar1=1.0 / DMODEL,
                                      scalar2=None, op0=ALU.mult),
                      e.scalar_tensor_tensor(out=mrs_r, in0=mean, scalar=-1.0,
                                             in1=mean, op0=ALU.mult, op1=ALU.mult),
                      e.tensor_add(out=work, in0=work, in1=mrs_r))[-1])
        st("scalar", ["work", "epsln"], ["work"],
           lambda e: (e.activation(out=lnv, in_=var, func=AF.Ln, bias=epsln[:]),
                      e.activation(out=rs_r, in_=lnv, func=AF.Exp, scale=-0.5))[-1])
        st("vector", ["work", "mean"], ["mrs_r"],
           lambda e: e.tensor_mul(out=mrs_r, in0=mean, in1=rs_r))

    sch.fence()
    ps_rs1 = Sp(nc.psum_tensor("ps_rs1", [P, TOKW], F32))
    ps_mrs = Sp(nc.psum_tensor("ps_mrs", [P, TOKW], F32))

    def mm_bcast(e):
        for (lo, hi) in NSL:
            e.matmul(out=ps_rs1[:, lo:hi], lhsT=ones_row[:], rhs=rs_r[:, lo:hi],
                     start=True, stop=True)
            i = e.matmul(out=ps_mrs[:, lo:hi], lhsT=ones_row[:],
                         rhs=mrs_r[:, lo:hi], start=True, stop=True)
        return i
    st("tensor", ["work", "mrs_r", "ones_row"], ["ps_rs1", "ps_mrs"], mm_bcast)

    for c in range(NCH):
        bn = f"xst{c % 3}"
        st("sync", [], [bn],
           (lambda e, c=c: e.dma_start(out=xst[c % 3][:, 0:P],
                                       in_=x_fm[c * P:(c + 1) * P, 0:P])),
           dma=bn)
        st("vector", [bn, "ps_rs1", "ps_mrs", f"acc{c}"], [f"xn{c}", "xsq"],
           (lambda e, c=c: (
               e.tensor_mul(out=xsq[:, 0:P], in0=xst[c % 3][:, 0:P],
                            in1=ps_rs1[:, 0:P]),
               e.tensor_sub(out=xn[:, c * TOKW:c * TOKW + P],
                            in0=xsq[:, 0:P], in1=ps_mrs[:, 0:P]),
               e.tensor_mul(out=xsq[:, P:TOKW],
                            in0=acc[:, c * NTOK:(c + 1) * NTOK],
                            in1=ps_rs1[:, P:TOKW]),
               e.tensor_sub(out=xn[:, c * TOKW + P:(c + 1) * TOKW],
                            in0=xsq[:, P:TOKW],
                            in1=ps_mrs[:, P:TOKW]))[-1]))
    if DEBUG:
        st("gpsimd", [f"acc{m}" for m in range(16)], [],
           lambda e: e.dma_start(out=dbg["acc1"], in_=acc[:]), dma="dbg")
        st("gpsimd", xnr, [],
           lambda e: e.dma_start(out=dbg["xn"], in_=xn[:]), dma="dbg")
    ph1.close()
    sch.fence()

    qkv_r = qkv_wT_d.rearrange("(c p) q -> c p q", p=P)
    out_r = out_wT_d.rearrange("(c p) q -> c p q", p=P)
    w1_r = w1T_d.rearrange("(c p) q -> c p q", p=P)
    w2_r = w2T_d.rearrange("(c p) q -> c p q", p=P)

    def load_wslab(src_r, colbase, width, half, wt):
        for cp in range(4):
            st("gpsimd", [], [f"wbf{half}"],
               (lambda e, cp=cp, colbase=colbase, width=width, src_r=src_r,
                        w=wt[half]:
                e.dma_start(
                    out=w[:, 4 * cp * width:(4 * cp + 4) * width].rearrange(
                        "p (c q) -> p c q", c=4),
                    in_=src_r[4 * cp:4 * cp + 4, :, colbase:colbase + width]
                        .rearrange("c p q -> p c q"))),
               dma=f"wld{half}")

    if PHASES < 2:
        st("gpsimd", [f"acc{m}" for m in range(16)] + xnr, ["odram"],
           lambda e: e.dma_start(
               out=o_fm.rearrange("(c p) q -> c p q", p=P).rearrange("c p q -> p c q"),
               in_=acc[:].rearrange("p (c q) -> p c q", c=NCH)),
           dma="ost")
        fv = sch.dma["ost"][1]
        sch.prog["gpsimd"].append(
            lambda e, fv=fv: e.wait_ge(sch.dma["ost"][0], fv))
        sch.emit()
        root.close()
        return nc, sch

    ph23 = ExitStack()
    S23 = ph23.enter_context
    k_both = S23(nc.sbuf_tensor("k_both", [P, NKV * TOKW], BF16))
    v_t = S23(nc.sbuf_tensor("v_t", [P, 9 * 512], BF16))
    qcost = S23(nc.sbuf_tensor("qcost", [P, NTOK], F32))
    qsint = S23(nc.sbuf_tensor("qsint", [P, NTOK], F32))
    st("sync", [], ["qcost"],
       lambda e: e.dma_start(out=qcost[:], in_=qcos_d), dma="tab_qcost")
    st("sync", [], ["qsint"],
       lambda e: e.dma_start(out=qsint[:], in_=qsin_d), dma="tab_qsint")
    ph2 = ExitStack()
    Sp = ph2.enter_context
    cost = Sp(nc.sbuf_tensor("cost", [P, TOKW], F32))
    sint = Sp(nc.sbuf_tensor("sint", [P, TOKW], F32))
    st("sync", [], ["cost"],
       lambda e: e.dma_start(out=cost[:], in_=cos_d), dma="tab_cost")
    st("sync", [], ["sint"],
       lambda e: e.dma_start(out=sint[:], in_=sin_d), dma="tab_sint")
    kf32 = Sp(nc.sbuf_tensor("kf32", [P, TOKW], F32))
    t1s = Sp(nc.sbuf_tensor("t1s", [P, TOKW], F32))
    krtmp = Sp(nc.sbuf_tensor("krtmp", [P, TOKW], BF16))
    ps_k = Sp(nc.psum_tensor("ps_k", [P, TOKW], F32))
    ps_rot = Sp(nc.psum_tensor("ps_rot", [P, TOKW], F32))
    ps_v = Sp(nc.psum_tensor("ps_v", [P, 512], F32))
    wbf2 = [Sp(nc.sbuf_tensor(f"wbf2_{i}", [P, 8192], BF16)) for i in range(2)]

    load_wslab(qkv_r, 2048, 512, 0, wbf2)
    load_wslab(qkv_r, 2560, 512, 1, wbf2)

    for m in range(4):
        def mm_k(e, m=m, w=None):
            w = wbf2[0]
            for c in range(NCH):
                for (lo, hi) in NSL:
                    i = e.matmul(
                        out=ps_k[:, lo:hi],
                        lhsT=w[:, c * 512 + m * P:c * 512 + (m + 1) * P],
                        rhs=xn[:, c * TOKW + lo:c * TOKW + hi],
                        start=(c == 0), stop=(c == NCH - 1))
            return i
        st("tensor", ["wbf0"] + xnr, ["ps_k"], mm_k)
        st("scalar", ["ps_k", "qkvbT"], ["kf32"],
           (lambda e, m=m: e.activation(out=kf32[:], in_=ps_k[:], func=AF.Identity,
                                        bias=qkvbT[:, 16 + m:17 + m])))

        def mm_rot(e):
            for (lo, hi) in NSL:
                i = e.matmul(out=ps_rot[:, lo:hi], lhsT=spermt[:],
                             rhs=kf32[:, lo:hi], start=True, stop=True)
            return i
        st("tensor", ["kf32", "spermt"], ["ps_rot"], mm_rot)
        st("vector", ["kf32", "ps_rot", "cost", "sint"], ["krtmp", "t1s", "kf32"],
           (lambda e, m=m: (
               e.tensor_mul(out=t1s[:], in0=kf32[:], in1=cost[:]),
               e.tensor_mul(out=kf32[:], in0=ps_rot[:], in1=sint[:]),
               e.tensor_add(out=krtmp[:], in0=t1s[:], in1=kf32[:]))[-1]))
        for gg in range(2):
            g = 2 * m + gg
            bsrc = 64 * gg
            st("vector", ["krtmp"], [f"kb{g}"],
               (lambda e, g=g, bsrc=bsrc: (
                   e.tensor_copy(out=k_both[0:64, g * TOKW:(g + 1) * TOKW],
                                 in_=krtmp[bsrc:bsrc + 64, :]),
                   e.tensor_copy(out=k_both[64:128, g * TOKW:(g + 1) * TOKW],
                                 in_=krtmp[bsrc:bsrc + 64, :]))[-1]))
    for tt in range(9):
        def mm_v(e, tt=tt):
            for c in range(NCH):
                e.matmul(out=ps_v[:],
                         lhsT=xn[:, c * TOKW + tt * P:c * TOKW + (tt + 1) * P],
                         rhs=wbf2[1][:, c * 512:(c + 1) * 512],
                         start=(c == 0), stop=False)
            return e.matmul(out=ps_v[:], lhsT=ones_row[:], rhs=vbias_t[:],
                            start=False, stop=True)
        st("tensor", ["wbf1", "vbias", "ones_row"] + xnr, ["ps_v"], mm_v)
        st("scalar", ["ps_v"], [f"vt{tt}"],
           (lambda e, tt=tt: e.copy(out=v_t[:, tt * 512:(tt + 1) * 512],
                                    in_=ps_v[:])))
    if DEBUG:
        st("gpsimd", [f"kb{g}" for g in range(NKV)], [],
           lambda e: e.dma_start(out=dbg["kb"], in_=k_both[:]), dma="dbg")
        st("gpsimd", [f"vt{t}" for t in range(9)], [],
           lambda e: e.dma_start(out=dbg["vt"], in_=v_t[:]), dma="dbg")
    ph2.close()
    sch.fence()

    if PHASES < 3:
        ph23.close()
        st("gpsimd", [f"acc{m}" for m in range(16)] + [f"kb{g}" for g in range(8)] + [f"vt{t}" for t in range(9)], ["odram"],
           lambda e: e.dma_start(
               out=o_fm.rearrange("(c p) q -> c p q", p=P).rearrange("c p q -> p c q"),
               in_=acc[:].rearrange("p (c q) -> p c q", c=NCH)),
           dma="ost")
        fv = sch.dma["ost"][1]
        sch.prog["gpsimd"].append(
            lambda e, fv=fv: e.wait_ge(sch.dma["ost"][0], fv))
        sch.emit()
        root.close()
        return nc, sch

    ph3 = ExitStack()
    Sp = ph3.enter_context
    qf32 = Sp(nc.sbuf_tensor("qf32", [P, 512], F32))
    t1q = Sp(nc.sbuf_tensor("t1q", [P, 512], F32))
    qr = Sp(nc.sbuf_tensor("qr", [P, 1024], BF16))
    qz = Sp(nc.sbuf_tensor("qz", [P, 4096], BF16))
    attn_h = Sp(nc.sbuf_tensor("attn_h", [P, NCH * 512], BF16))
    praw = Sp(nc.sbuf_tensor("praw", [P, 2048], BF16))
    pt_sb = Sp(nc.sbuf_tensor("pt_sb", [P, 1024], BF16))
    smal = Sp(nc.sbuf_tensor("smal", [P, 40], F32))
    ps_q = Sp(nc.psum_tensor("ps_q", [P, 512], F32))
    ps_rq = Sp(nc.psum_tensor("ps_rq", [P, 512], F32))
    ps_s = Sp(nc.psum_tensor("ps_s", [P, 2048], F32))
    ps_t = Sp(nc.psum_tensor("ps_t", [P, 1024], BF16))
    ps_av = Sp(nc.psum_tensor("ps_av", [P, 256], F32))
    ps_o = ps_rq
    wbf3 = [Sp(nc.sbuf_tensor(f"wbf3_{i}", [P, 4096], BF16)) for i in range(2)]

    st("vector", [], ["praw0", "praw1"], lambda e: e.memset(praw[:], 0.0))
    st("vector", [], ["qz0", "qz1"], lambda e: e.memset(qz[:], 0.0))

    pend = []

    for half in range(2):
        hlo = HALO + 512 * half
        def qprep_pieces(g, hlo=hlo, half=half):
            gp_ = g % 2
            wq = g % 2
            pcs = []
            for mq in range(2):
                def p_mm(mq=mq, wq=wq, hlo=hlo, g=g):
                    def mm_q(e, mq=mq, hlo=hlo, wq=wq):
                        for c in range(NCH):
                            i = e.matmul(
                                out=ps_q[:],
                                lhsT=wbf3[wq][:, c * 256 + mq * P:
                                              c * 256 + (mq + 1) * P],
                                rhs=xn[:, c * TOKW + hlo:c * TOKW + hlo + 512],
                                start=(c == 0), stop=(c == NCH - 1))
                        return i
                    st("tensor", [f"wbf{wq}"] + xnr, ["ps_q"], mm_q)
                    st("scalar", ["ps_q", "qkvbT"], ["qf32"],
                       (lambda e, g=g, mq=mq: e.activation(
                           out=qf32[:], in_=ps_q[:],
                           func=AF.Identity,
                           bias=qkvbT[:, 2 * g + mq:2 * g + mq + 1])))

                def p_rope(mq=mq, half=half, gp_=gp_, last=(mq == 1)):
                    st("tensor", ["qf32", "spermt"], ["ps_rq"],
                       lambda e: e.matmul(out=ps_rq[:], lhsT=spermt[:],
                                          rhs=qf32[:], start=True, stop=True))
                    st("vector", ["qf32", "ps_rq", "qcost", "qsint"],
                       [f"qr{mq}", "t1q", "qf32"],
                       (lambda e, mq=mq, half=half: (
                           e.tensor_mul(out=t1q[:], in0=qf32[:],
                                        in1=qcost[:, half * 512:(half + 1) * 512]),
                           e.tensor_mul(out=qf32[:], in0=ps_rq[:],
                                        in1=qsint[:, half * 512:(half + 1) * 512]),
                           e.tensor_add(out=qr[:, mq * 512:(mq + 1) * 512],
                                        in0=t1q[:], in1=qf32[:]))[-1]))
                    if last:
                        def do_qz(e, gp_=gp_):
                            for h in range(4):
                                hb = 64 * (h % 2)
                                i = e.tensor_copy(
                                    out=qz[hb:hb + 64,
                                           2048 * gp_ + 512 * h:
                                           2048 * gp_ + 512 * (h + 1)],
                                    in_=qr[hb:hb + 64,
                                           (h // 2) * 512:(h // 2 + 1) * 512])
                            return i
                        st("vector", ["qr0", "qr1"], [f"qz{gp_}"], do_qz)
                pcs.append(p_mm)
                pcs.append(p_rope)
            return pcs

        qpp = []
        for g in range(NKV):
            gp_ = g % 2
            if g == 0:
                load_wslab(qkv_r, 0, 256, 0, wbf3)
                for t_ in qprep_pieces(0):
                    t_()
            if g + 1 < NKV:
                load_wslab(qkv_r, 256 * (g + 1), 256, (g + 1) % 2, wbf3)
                qpp = qprep_pieces(g + 1)
            else:
                qpp = []
            for qt in range(4):
                if PH3_LEVEL < 1:
                    continue
                aqt = 4 * half + qt
                par = (4 * g + qt) % 2
                so = 1024 * par
                sm0 = 20 * par
                m_neg = smal[:, sm0:sm0 + 4]
                rowsum = smal[:, sm0 + 4:sm0 + 8]
                es_t = smal[:, sm0 + 8:sm0 + 12]
                den_t = smal[:, sm0 + 12:sm0 + 16]
                rden = smal[:, sm0 + 16:sm0 + 20]

                def mm_s(e, g=g, qt=qt, aqt=aqt, gp_=gp_, so=so):
                    for h in range(4):
                        kt = k_both[:, g * TOKW:(g + 1) * TOKW]
                        i = e.matmul(
                            out=ps_s[:, so + h * 256:so + (h + 1) * 256],
                            lhsT=qz[:, 2048 * gp_ + 512 * h + qt * P:
                                    2048 * gp_ + 512 * h + (qt + 1) * P],
                            rhs=kt[:, P * aqt:P * aqt + 256],
                            start=True, stop=True)
                    return i
                st("tensor", [f"qz{gp_}", f"kb{g}"], [f"ps_s{par}"], mm_s)
                if PH3_LEVEL < 2:
                    continue
                if pend:
                    pend[0]()
                st("vector", [f"ps_s{par}"], [f"m_neg{par}"],
                   (lambda e, so=so, m_neg=m_neg: e.tensor_reduce(
                       out=m_neg,
                       in_=ps_s[:, so:so + 1024].rearrange(
                           "p (h k) -> p h k", k=256),
                       axis=AX.X, op=ALU.max, negate=True)))
                for th_ in pend[1:]:
                    th_()
                pend = []

                def do_exp(e, so=so, m_neg=m_neg):
                    for h in range(4):
                        i = e.activation(
                            out=praw[:, so + h * 256:so + (h + 1) * 256],
                            in_=ps_s[:, so + h * 256:so + (h + 1) * 256],
                            func=AF.Exp, bias=m_neg[:, h:h + 1])
                    return i
                st("scalar", [f"ps_s{par}", f"m_neg{par}"], [f"praw{par}"], do_exp)

                def do_es(e, g=g, m_neg=m_neg, es_t=es_t):
                    for h in range(4):
                        i = e.activation(out=es_t[:, h:h + 1],
                                         in_=m_neg[:, h:h + 1], func=AF.Exp,
                                         bias=sinks_t[:, 4 * g + h:4 * g + h + 1])
                    return i
                st("scalar", [f"m_neg{par}", "sinks"], [f"es_t{par}"], do_es)
                mkn = "maskPD0" if aqt == 0 else "maskPD"
                mk = maskPD0 if aqt == 0 else maskPD

                def do_mask(e, mk=mk, so=so, rowsum=rowsum):
                    for h in range(4):
                        i = e.scalar_tensor_tensor(
                            out=praw[:, so + h * 256:so + (h + 1) * 256],
                            in0=praw[:, so + h * 256:so + (h + 1) * 256],
                            scalar=1.0,
                            in1=mk[:], op0=ALU.mult, op1=ALU.mult,
                            accum_out=rowsum[:, h:h + 1])
                    return i
                st("vector", [f"praw{par}", mkn],
                   [f"praw{par}", f"rowsum{par}"], do_mask)
                st("vector", [f"rowsum{par}", f"es_t{par}"], [f"den_t{par}"],
                   (lambda e, den_t=den_t, rowsum=rowsum, es_t=es_t:
                    e.tensor_add(out=den_t, in0=rowsum, in1=es_t)))
                st("vector", [f"den_t{par}"], [f"rden{par}"],
                   (lambda e, rden=rden, den_t=den_t:
                    e.reciprocal(out=rden, in_=den_t)))

                def do_norm(e, so=so, rden=rden):
                    for h in range(4):
                        i = e.tensor_scalar(
                            out=praw[:, so + h * 256:so + (h + 1) * 256],
                            in0=praw[:, so + h * 256:so + (h + 1) * 256],
                            scalar1=rden[:, h:h + 1],
                            scalar2=None, op0=ALU.mult)
                    return i
                st("vector", [f"rden{par}", f"praw{par}"], [f"praw{par}"],
                   do_norm)
                if PH3_LEVEL < 3:
                    continue

                def do_pt(e, so=so):
                    for h in range(4):
                        for k2 in range(2):
                            i = e.transpose(
                                out=ps_t[:, k2 * 512 + h * P:k2 * 512 + (h + 1) * P],
                                in_=praw[:, so + h * 256 + k2 * P:
                                         so + h * 256 + (k2 + 1) * P],
                                identity=identt[:])
                    return i

                def do_av(e, g=g, aqt=aqt):
                    for h in (0, 2, 1, 3):
                        ob = 64 * (h % 2)
                        oc = P * (h // 2)
                        e.matmul(out=ps_av[ob:ob + 64, oc:oc + P],
                                 lhsT=v_t[:, aqt * 512 + 64 * g:aqt * 512 + 64 * g + 64],
                                 rhs=pt_sb[:, h * P:(h + 1) * P],
                                 start=True, stop=False)
                        i = e.matmul(
                            out=ps_av[ob:ob + 64, oc:oc + P],
                            lhsT=v_t[:, (aqt + 1) * 512 + 64 * g:
                                     (aqt + 1) * 512 + 64 * g + 64],
                            rhs=pt_sb[:, 512 + h * P:512 + (h + 1) * P],
                            start=False, stop=True)
                    return i

                def do_avev(e, g=g, qt=qt):
                    for hh in range(2):
                        i = e.copy(
                            out=attn_h[:, (2 * g + hh) * 512 + qt * P:
                                       (2 * g + hh) * 512 + (qt + 1) * P],
                            in_=ps_av[:, hh * P:(hh + 1) * P])
                    return i
                pend = [
                    (lambda par=par, do_pt=do_pt: st(
                        "tensor", [f"praw{par}", "identt"], ["ps_t"], do_pt)),
                    (lambda: st("vector", ["ps_t"], ["pt_sb"],
                                lambda e: e.tensor_copy(out=pt_sb[:],
                                                        in_=ps_t[:]))),
                    (lambda aqt=aqt, do_av=do_av: st(
                        "tensor", ["pt_sb", f"vt{aqt}", f"vt{aqt + 1}"],
                        ["ps_av"], do_av)),
                    (lambda g=g, do_avev=do_avev: st(
                        "scalar", ["ps_av"],
                        [f"attnh{2 * g}", f"attnh{2 * g + 1}"], do_avev)),
                ]
                if qpp:
                    qpp.pop(0)()
            while qpp:
                qpp.pop(0)()
        if PH3_LEVEL < 5:
            continue
        for th_ in pend:
            th_()
        pend = []
        if DEBUG:
            st("gpsimd", [f"attnh{i}" for i in range(16)], [],
               (lambda e, half=half: e.dma_start(out=dbg["attn" + str(half)],
                                                 in_=attn_h[:])), dma="dbg")
        ares = [f"attnh{i}" for i in range(16)]
        for sl in range(8):
            wo = sl % 2
            load_wslab(out_r, 256 * sl, 256, wo, wbf3)
            for dm in (2 * sl, 2 * sl + 1):
                pst = ps_q if dm % 2 else ps_rq
                psn = "ps_q" if dm % 2 else "ps_rq"

                def mm_o(e, dm=dm, pst=pst, wo=wo):
                    for c in range(NCH):
                        i = e.matmul(
                            out=pst[:],
                            lhsT=wbf3[wo][:, c * 256 + (dm % 2) * P:
                                          c * 256 + (dm % 2 + 1) * P],
                            rhs=attn_h[:, c * 512:(c + 1) * 512],
                            start=(c == 0), stop=(c == NCH - 1))
                    return i
                st("tensor", [f"wbf{wo}"] + ares, [psn], mm_o)
                st("vector", [psn, f"acc{dm}"], [f"acc{dm}"],
                   (lambda e, dm=dm, half=half, pst=pst: e.tensor_add(
                       out=acc[:, dm * NTOK + 512 * half:
                               dm * NTOK + 512 * (half + 1)],
                       in0=acc[:, dm * NTOK + 512 * half:
                               dm * NTOK + 512 * (half + 1)],
                       in1=pst[:])))
    for dm in range(16):
        st("vector", [f"acc{dm}", "outbC"], [f"acc{dm}"],
           (lambda e, dm=dm: e.tensor_scalar(
               out=acc[:, dm * NTOK:(dm + 1) * NTOK],
               in0=acc[:, dm * NTOK:(dm + 1) * NTOK],
               scalar1=outbC[:, dm:dm + 1], scalar2=None, op0=ALU.add)))
    if DEBUG:
        st("gpsimd", [f"acc{m}" for m in range(16)], [],
           lambda e: e.dma_start(out=dbg["acc2"], in_=acc[:]), dma="dbg")
    ph3.close()
    ph23.close()
    sch.fence()

    if PHASES < 4:
        st("gpsimd", [f"acc{m}" for m in range(16)], ["odram"],
           lambda e: e.dma_start(
               out=o_fm.rearrange("(c p) q -> c p q", p=P).rearrange("c p q -> p c q"),
               in_=acc[:].rearrange("p (c q) -> p c q", c=NCH)),
           dma="ost")
        fv = sch.dma["ost"][1]
        sch.prog["gpsimd"].append(
            lambda e, fv=fv: e.wait_ge(sch.dma["ost"][0], fv))
        sch.emit()
        root.close()
        return nc, sch

    ph4 = ExitStack()
    Sp = ph4.enter_context
    wbf4 = [Sp(nc.sbuf_tensor(f"wbf4_{i}", [P, 8192], BF16)) for i in range(2)]
    gpt = Sp(nc.sbuf_tensor("gpt", [P, 512], BF16))
    epbt2 = Sp(nc.sbuf_tensor("epbt2", [64, 2048], BF16))
    epbt = epbt2[0:32, :]
    qabs = [Sp(nc.sbuf_tensor(f"qabs{i}", [P, 512], BF16)) for i in range(2)]
    scal_t = Sp(nc.sbuf_tensor("scal_t", [32, 512], BF16))
    rs_t = Sp(nc.sbuf_tensor("rs_t", [32, 512], F32))
    qp_t = Sp(nc.sbuf_tensor("qp_t", [P, 512], BF16))
    rs_hl = Sp(nc.sbuf_tensor("rs_hl", [64, 512], BF16))
    rs_lo = Sp(nc.sbuf_tensor("rs_lo", [32, 512], BF16))
    zp_t = Sp(nc.sbuf_tensor("zp_t", [P, 512], BF16))
    zn_t = Sp(nc.sbuf_tensor("zn_t", [P, 512], BF16))
    u_sb = Sp(nc.sbuf_tensor("u_sb", [P, 4 * 1024], BF16))
    u2_sb = Sp(nc.sbuf_tensor("u2_sb", [P, 1024], BF16))
    w2ob = Sp(nc.sbuf_tensor("w2ob", [P, 16 * 512], BF16))
    rstd_r = Sp(nc.sbuf_tensor("rstd_r", [1, NTOK], F32))
    ps_g4 = Sp(nc.psum_tensor("ps_g4", [32, 512], F32))
    ps_rs4 = Sp(nc.psum_tensor("ps_rs4", [P, 1024], F32))
    ps_sc4 = Sp(nc.psum_tensor("ps_sc4", [P, 1024], F32))
    ps_h = Sp(nc.psum_tensor("ps_h", [P, 1024], F32))
    ps_ms = Sp(nc.psum_tensor("ps_ms", [P, 512], F32))
    ms_acc = Sp(nc.sbuf_tensor("ms_acc", [1, NTOK], F32))
    st("sync", [], ["gpt"],
       lambda e: e.dma_start(out=gpt[:], in_=gp_d), dma="tab_gpt")
    st("sync", [], ["epbt", "epbt2"],
       lambda e: e.dma_start(out=epbt2[:], in_=epb2_d), dma="tab_epbt2")

    def quant_batch(src_r, colbase, chunk0, dst, out_view_fn, bufkey,
                    after_apply=None, width=512, hook=None, hook2=None):
        for cp in range(4):
            rns = sorted({out_view_fn(chunk0 + 4 * cp + j)[1] for j in range(4)})
            st("gpsimd", [], rns,
               (lambda e, cp=cp, chunk0=chunk0, colbase=colbase, src_r=src_r,
                        width=width, dst=dst:
                e.dma_start(
                    out=dst[:, 4 * cp * width:(4 * cp + 4) * width].rearrange(
                        "p (c q) -> p c q", c=4),
                    in_=src_r[chunk0 + 4 * cp:chunk0 + 4 * cp + 4, :,
                              colbase:colbase + width]
                        .rearrange("c p q -> p c q"))),
               dma=f"wq{bufkey}_{cp}")
        for c in range(16):
            cg = chunk0 + c
            ov, orn = out_view_fn(cg)
            if hook is not None and c % 4 == 0:
                hook()
            st("scalar", [orn], [f"qabs{c % 2}"],
               (lambda e, ov=ov, c=c, width=width:
                e.activation(out=qabs[c % 2][:, :width], in_=ov, func=AF.Abs)))
            st("tensor", [f"qabs{c % 2}", "gpt"], ["ps_g4"],
               (lambda e, c=c, width=width: e.matmul(
                   out=ps_g4[:, :width], lhsT=gpt[:, 32 * c:32 * (c + 1)],
                   rhs=qabs[c % 2][:, :width],
                   start=(c == 0), stop=(c == 15))))
            if hook2 is not None and c % 4 == 3:
                hook2()
        st("vector", ["ps_g4"], ["scal_t", "rs_t", "rs_hl", "rs_lo"],
           (lambda e, width=width: (
               e.tensor_scalar(out=scal_t[:, :width], in0=ps_g4[:, :width],
                               scalar1=1.0 / 64, scalar2=1e-8,
                               op0=ALU.mult, op1=ALU.max),
               e.reciprocal(out=rs_t[:, :width], in_=scal_t[:, :width]),
               e.tensor_copy(out=rs_hl[0:32, :width], in_=rs_t[:, :width]),
               e.tensor_sub(out=rs_lo[:, :width], in0=rs_t[:, :width],
                            in1=rs_hl[0:32, :width]),
               e.tensor_copy(out=rs_hl[32:64, :width],
                             in_=rs_lo[:, :width]))[-1]))
        for c in range(16):
            cg = chunk0 + c
            ov, orn = out_view_fn(cg)
            pp = c % 2
            o0 = 512 * pp
            if hook is not None and c % 4 == 0:
                hook()

            def mm_exp(e, c=c, width=width, o0=o0):
                e.matmul(out=ps_rs4[:, o0:o0 + width],
                         lhsT=epbt2[:, P * c:P * (c + 1)],
                         rhs=rs_hl[:, :width], start=True, stop=True)
                return e.matmul(out=ps_sc4[:, o0:o0 + width],
                                lhsT=epbt2[0:32, P * c:P * (c + 1)],
                                rhs=scal_t[:, :width], start=True, stop=True)
            st("tensor", ["rs_hl", "scal_t", "epbt", "epbt2"],
               [f"ps_rs4{pp}", f"ps_sc4{pp}"], mm_exp)

            def do_apply(e, ov=ov, width=width, o0=o0, pp=pp):
                e.tensor_mul(out=qp_t[:, :width], in0=ov,
                             in1=ps_rs4[:, o0:o0 + width])
                e.tensor_scalar(out=zp_t[:, :width],
                                in0=qp_t[:, :width],
                                scalar1=0.5, scalar2=None, op0=ALU.is_gt)
                i = e.tensor_scalar(out=zn_t[:, :width],
                                    in0=qp_t[:, :width],
                                    scalar1=-0.5, scalar2=None, op0=ALU.is_lt)
                return i
            st("vector", [orn, f"ps_rs4{pp}"],
               [f"qzz{pp}"], do_apply)
            st("vector", [f"qzz{pp}", f"ps_sc4{pp}"], [orn],
               (lambda e, ov=ov, width=width, pp=pp, o0=o0: (
                   e.tensor_sub(out=zp_t[:, :width],
                                in0=zp_t[:, :width],
                                in1=zn_t[:, :width]),
                   e.tensor_mul(out=ov, in0=zp_t[:, :width],
                                in1=ps_sc4[:, o0:o0 + width]))[-1]))
            if after_apply is not None:
                after_apply(cg)
            if hook2 is not None and c % 4 == 3:
                hook2()

    def w2_view(cg):
        cc = cg % 16
        return w2ob[:, cc * 512:(cc + 1) * 512], f"w2o{cc}"

    def fc_pieces(s):
        half = s % 2
        msp = []
        thunks = []
        for m in range(4):
            for th in range(2):
                def piece_mm(s=s, m=m, th=th, half=half, msp=msp):
                    pb = (2 * m + th) % 2

                    def mm_fc(e, m=m, th=th, pb=pb, half=half):
                        for c in range(NCH):
                            i = e.matmul(
                                out=ps_h[:, pb * 512:(pb + 1) * 512],
                                lhsT=wbf4[half][:, c * 512 + m * P:
                                                c * 512 + (m + 1) * P],
                                rhs=xn[:, c * TOKW + HALO + th * 512:
                                       c * TOKW + HALO + (th + 1) * 512],
                                start=(c == 0), stop=(c == NCH - 1))
                        return i
                    st("tensor",
                       [f"w1c{half}_{cc}" for cc in range(16)] + xnr,
                       [f"ps_h{pb}"], mm_fc)
                    for f in msp:
                        f()
                    msp.clear()

                def piece(s=s, m=m, th=th, half=half, msp=msp):
                    pb = (2 * m + th) % 2
                    st("vector", [f"ps_h{pb}"], [f"u{m}"],
                       (lambda e, m=m, th=th, pb=pb: e.tensor_scalar(
                           out=u_sb[:, m * 1024 + th * 512:m * 1024 + (th + 1) * 512],
                           in0=ps_h[:, pb * 512:(pb + 1) * 512],
                           scalar1=0.0, scalar2=None, op0=ALU.max)))
                    st("scalar", [f"u{m}"], [f"u{m}"],
                       (lambda e, m=m, th=th: e.activation(
                           out=u_sb[:, m * 1024 + th * 512:m * 1024 + (th + 1) * 512],
                           in_=u_sb[:, m * 1024 + th * 512:m * 1024 + (th + 1) * 512],
                           func=AF.Square)))
                    st("scalar", [f"u{m}"], [f"u2{th}"],
                       (lambda e, m=m, th=th: e.activation(
                           out=u2_sb[:, th * 512:(th + 1) * 512],
                           in_=u_sb[:, m * 1024 + th * 512:m * 1024 + (th + 1) * 512],
                           func=AF.Square)))
                    first = (s == 0 and m == 0)
                    last = (s == W1_SLABS - 1 and m == 3)
                    msp.append(lambda th=th, first=first, last=last: st(
                        "tensor", [f"u2{th}", "onesb"], ["ps_ms"],
                        (lambda e, th=th, first=first, last=last: e.matmul(
                            out=ps_ms[64 * th:64 * th + 1, :],
                            lhsT=onesb[:],
                            rhs=u2_sb[:, th * 512:(th + 1) * 512],
                            start=first, stop=last))))
                    if th == 1:
                        hid0 = 512 * s + m * P
                        st("gpsimd", [f"u{m}"], [f"udram{s}_{m}"],
                           (lambda e, hid0=hid0, m=m: e.dma_start(
                               out=u_scr[hid0:hid0 + P, :],
                               in_=u_sb[:, m * 1024:(m + 1) * 1024])),
                           dma=f"ust{m}")
                thunks.append((piece_mm, piece))

        def final(msp=msp):
            for f in msp:
                f()
            msp.clear()
        return thunks, final

    fcq = []
    restq = []
    fin_prev = None

    def hk():
        if fcq:
            mm_t, rest_t = fcq.pop(0)
            mm_t()
            restq.append(rest_t)

    def hk2():
        if restq:
            restq.pop(0)()

    for s in range(W1_SLABS):
        half = s % 2

        def w1_view(c, half=half):
            return wbf4[half][:, c * 512:(c + 1) * 512], f"w1c{half}_{c}"
        quant_batch(w1_r, 512 * s, 0, wbf4[half], w1_view, f"w1{half}",
                    hook=hk, hook2=hk2)
        while fcq or restq:
            hk()
            hk2()
        if fin_prev is not None:
            fin_prev()
        ws = s // 4
        wb_ = s % 4

        def w2_store(cg, ws=ws):
            cc = cg % 16
            st("gpsimd", [f"w2o{cc}"], [f"w2qdram{ws}_{cg}"],
               (lambda e, ws=ws, cg=cg, cc=cc: e.dma_start(
                   out=w2q_scr[P * cg:P * (cg + 1), 512 * ws:512 * (ws + 1)],
                   in_=w2ob[:, cc * 512:(cc + 1) * 512])),
               dma=f"w2st{cc % 4}")
        quant_batch(w2_r, 512 * ws, 16 * wb_, w2ob, w2_view, "w2",
                    after_apply=w2_store)
        fcq, fin_prev = fc_pieces(s)
    while fcq or restq:
        hk()
        hk2()
    if fin_prev is not None:
        fin_prev()
    st("vector", ["ps_ms"], ["ms_acc"],
       lambda e: (e.tensor_copy(out=ms_acc[:, 0:512], in_=ps_ms[0:1, :]),
                  e.tensor_copy(out=ms_acc[:, 512:1024], in_=ps_ms[64:65, :]))[-1])

    st("scalar", ["ms_acc", "epsrms"], ["rstd_r"],
       lambda e: (e.activation(out=rstd_r[:], in_=ms_acc[:], func=AF.Ln,
                               bias=epsrms[:], scale=1.0 / 8192),
                  e.activation(out=rstd_r[:], in_=rstd_r[:], func=AF.Exp,
                               scale=-0.5))[-1])

    def mm_rstd(e):
        e.matmul(out=ps_h[:, 0:512], lhsT=ones_row[:], rhs=rstd_r[:, 0:512],
                 start=True, stop=True)
        return e.matmul(out=ps_h[:, 512:1024], lhsT=ones_row[:],
                        rhs=rstd_r[:, 512:1024], start=True, stop=True)
    st("tensor", ["rstd_r", "ones_row"], ["ps_h0", "ps_h1"], mm_rstd)
    st("vector", ["ps_h0", "ps_h1"], ["rstd128"],
       lambda e: e.tensor_copy(out=rstd128[:], in_=ps_h[:, 0:NTOK]))
    if DEBUG:
        st("gpsimd", ["rstd128"], [],
           lambda e: e.dma_start(out=dbg["rstd"], in_=rstd128[:]), dma="dbg")

    ph4.close()
    sch.fence()

    if PHASES < 5:
        st("gpsimd", [f"acc{m}" for m in range(16)] + ["rstd128"], ["odram"],
           lambda e: e.dma_start(
               out=o_fm.rearrange("(c p) q -> c p q", p=P).rearrange("c p q -> p c q"),
               in_=acc[:].rearrange("p (c q) -> p c q", c=NCH)),
           dma="ost")
        fv = sch.dma["ost"][1]
        sch.prog["gpsimd"].append(
            lambda e, fv=fv: e.wait_ge(sch.dma["ost"][0], fv))
        sch.emit()
        root.close()
        return nc, sch

    ph5 = ExitStack()
    Sp = ph5.enter_context
    w2b = [Sp(nc.sbuf_tensor(f"w2b{i}", [P, 4 * 512], BF16)) for i in range(2)]
    u_ld = [Sp(nc.sbuf_tensor(f"u_ld{i}", [P, 4 * 1024], BF16)) for i in range(2)]
    tmul = Sp(nc.sbuf_tensor("tmul", [P, 1024], F32))
    ps_p = Sp(nc.psum_tensor("ps_p", [P, 4 * 1024], F32))
    w2q_r = w2q_scr.rearrange("(c p) q -> c p q", p=P)
    u_r = u_scr.rearrange("(c p) q -> c p q", p=P)

    uall = [f"udram{s}_{m}" for s in range(W1_SLABS) for m in range(4)]
    w2all = [f"w2qdram{s}_{c}" for s in range(W2_SLABS) for c in range(64)]

    for dq in range(4):
        for kg in range(16):
            b = kg % 2
            first = (dq == 0 and kg < 2)
            st("sync", w2all if first else [], [f"w2b{b}"],
               (lambda e, dq=dq, kg=kg, b=b: e.dma_start(
                   out=w2b[b][:].rearrange("p (c q) -> p c q", c=4),
                   in_=w2q_r[4 * kg:4 * kg + 4, :, 512 * dq:512 * (dq + 1)]
                       .rearrange("c p q -> p c q"))),
               dma=f"w2b{b}")
            st("sync", uall if first else [], [f"u_ld{b}"],
               (lambda e, kg=kg, b=b: e.dma_start(
                   out=u_ld[b][:].rearrange("p (c q) -> p c q", c=4),
                   in_=u_r[4 * kg:4 * kg + 4, :, :]
                       .rearrange("c p q -> p c q"))),
               dma=f"uld{b}")

            def mm_p(e, kg=kg, b=b):
                for cc in range(4):
                    for dm in range(4):
                        for th in range(2):
                            i = e.matmul(
                                out=ps_p[:, dm * 1024 + th * 512:
                                         dm * 1024 + (th + 1) * 512],
                                lhsT=w2b[b][:, cc * 512 + dm * P:cc * 512 + (dm + 1) * P],
                                rhs=u_ld[b][:, cc * 1024 + th * 512:
                                            cc * 1024 + (th + 1) * 512],
                                start=(kg == 0 and cc == 0),
                                stop=(kg == 15 and cc == 3))
                return i
            st("tensor", [f"w2b{b}", f"u_ld{b}"],
               [f"ps_p{d}" for d in range(4)], mm_p)
        for dm in range(4):
            adm = 4 * dq + dm
            st("vector", [f"ps_p{dm}", "rstd128", f"acc{adm}"],
               [f"acc{adm}", "tmul"],
               (lambda e, dm=dm, adm=adm: (
                   e.tensor_mul(out=tmul[:], in0=ps_p[:, dm * 1024:(dm + 1) * 1024],
                                in1=rstd128[:]),
                   e.tensor_add(out=acc[:, adm * NTOK:(adm + 1) * NTOK],
                                in0=acc[:, adm * NTOK:(adm + 1) * NTOK],
                                in1=tmul[:]))[-1]))
    ph5.close()

    st("gpsimd", [f"acc{m}" for m in range(16)], ["odram"],
       lambda e: e.dma_start(
           out=o_fm.rearrange("(c p) q -> c p q", p=P).rearrange("c p q -> p c q"),
           in_=acc[:].rearrange("p (c q) -> p c q", c=NCH)),
       dma="ost")
    fv = sch.dma["ost"][1]
    sch.prog["gpsimd"].append(
        lambda e, fv=fv: e.wait_ge(sch.dma["ost"][0], fv))

    sch.emit()
    root.close()
    return nc, sch


def kernel(L0_attn_x, L0_attn_qkv_w, L0_attn_qkv_b, L0_attn_out_w,
           L0_attn_out_b, L0_attn_sinks, L0_ffn_W_fc, L0_ffn_W_proj):
    sh, cores = _host_prep(L0_attn_x, L0_attn_qkv_w, L0_attn_qkv_b,
                           L0_attn_out_w, L0_attn_out_b, L0_attn_sinks,
                           L0_ffn_W_fc, L0_ffn_W_proj)
    nc, _ = build_nc()
    in_maps = []
    for cid in range(8):
        m = dict(sh)
        m.update(cores[cid])
        in_maps.append(m)
    res = run_bass_kernel_spmd(nc, in_maps, core_ids=list(range(8)))
    out = np.empty((2, 4096, DMODEL), np.float32)
    for cid in range(8):
        b, j = cid // 4, cid % 4
        out[b, j * NTOK:(j + 1) * NTOK] = np.asarray(res.results[cid]["o_fm"]).T
    return out


# version 20 (final)
# speedup vs baseline: 1.0681x; 1.0681x over previous
import math
import numpy as np
import ml_dtypes
from contextlib import ExitStack

import concourse.bass as bass
import concourse.mybir as mybir
from concourse.bass_utils import run_bass_kernel_spmd

F32 = mybir.dt.float32
BF16 = mybir.dt.bfloat16
AF = mybir.ActivationFunctionType
ALU = mybir.AluOpType
AX = mybir.AxisListType
bf = ml_dtypes.bfloat16

P = 128
DMODEL = 2048
NCH = 16
TOKW = 1152
NTOK = 1024
HALO = 128
NKV = 8
LN_EPS = 1e-5
RMS_EPS = 1.1920929e-07
W1_SLABS = 16
W2_SLABS = 4
DEBUG = False
PHASES = 99
PH3_LEVEL = 9


def _rope_cos_sin(t_len):
    HEAD_DIM = 64
    ROPE_THETA = 150000.0
    SCALING = 32.0
    NTK_ALPHA = 1.0
    NTK_BETA = 32.0
    ICL = 1024
    d_half = HEAD_DIM // 2
    freq = ROPE_THETA ** (np.arange(0, HEAD_DIM, 2, dtype=np.float32) / HEAD_DIM)
    concentration = 0.1 * math.log(SCALING) + 1.0
    low = d_half * math.log(ICL / (NTK_BETA * 2 * math.pi)) / math.log(ROPE_THETA)
    high = d_half * math.log(ICL / (NTK_ALPHA * 2 * math.pi)) / math.log(ROPE_THETA)
    interpolation = 1.0 / (SCALING * freq)
    extrapolation = 1.0 / freq
    ramp = np.clip((np.arange(d_half, dtype=np.float32) - low) / (high - low), 0.0, 1.0)
    mask = 1.0 - ramp
    inv_freq = interpolation * (1.0 - mask) + extrapolation * mask
    freqs = np.outer(np.arange(t_len, dtype=np.float32), inv_freq)
    return np.cos(freqs) * concentration, np.sin(freqs) * concentration


def _host_prep(x, qkv_w, qkv_b, out_w, out_b, sinks, W_fc, W_proj):
    f32 = np.float32
    sh = {}
    sh["qkv_wT"] = np.ascontiguousarray(qkv_w.T.astype(f32, copy=False))
    sh["qkv_bT"] = np.ascontiguousarray(
        qkv_b.astype(f32, copy=False)[:2560].reshape(20, P).T)
    sh["vbias"] = np.ascontiguousarray(
        qkv_b.astype(f32, copy=False)[2560:].reshape(1, 512))
    sh["out_wT"] = np.ascontiguousarray(out_w.T.astype(f32, copy=False))
    sh["out_bC"] = np.ascontiguousarray(
        out_b.astype(f32, copy=False).reshape(16, P).T)
    sh["sinks128"] = np.ascontiguousarray(
        np.broadcast_to(sinks.astype(f32, copy=False), (P, 32)))
    sh["w1T"] = np.ascontiguousarray(W_fc.T.astype(f32, copy=False))
    sh["w2T"] = np.ascontiguousarray(W_proj.T.astype(f32, copy=False))
    sh["ident"] = np.eye(P, dtype=bf)
    sh["ones_col"] = np.ones((P, 1), f32)
    sh["ones_row"] = np.ones((1, P), f32)
    sperm = np.zeros((P, P), f32)
    for m in range(P):
        if m % 64 < 32:
            sperm[m + 32, m] = -1.0
        else:
            sperm[m - 32, m] = 1.0
    sh["sperm"] = sperm
    r = np.arange(P)[:, None]
    c = np.arange(P)[None, :]
    maskP = (c > r).astype(bf)
    maskD = (c <= r).astype(bf)
    sh["maskPD"] = np.ascontiguousarray(np.concatenate([maskP, maskD], axis=1))
    maskPD0 = sh["maskPD"].copy()
    maskPD0[:, :P] = 0
    gp = np.zeros((P, 16 * 32), dtype=bf)
    for cc in range(16):
        for p in range(P):
            gp[p, 32 * cc + 2 * cc + p // 64] = 1.0
    sh["gp"] = gp
    ep = np.zeros((32, 16 * P), f32)
    for cc in range(16):
        for p in range(P):
            ep[2 * cc + p // 64, P * cc + p] = 1.0
    sh["ep"] = ep
    sh["epb"] = ep.astype(bf)
    sh["epb2"] = np.concatenate([ep, ep], axis=0).astype(bf)

    cos_t, sin_t = _rope_cos_sin(4096)
    cores = []
    for cid in range(8):
        b, j = cid // 4, cid % 4
        t0 = NTOK * j
        d = {}
        xb = x[b].astype(f32, copy=False)
        xc = np.zeros((TOKW, DMODEL), f32)
        lo = max(0, t0 - HALO)
        xc[HALO - (t0 - lo):] = xb[lo:t0 + NTOK]
        d["x_fm"] = np.ascontiguousarray(xc.T)
        pos = np.arange(t0 - HALO, t0 + NTOK)
        posc = np.clip(pos, 0, 4095)
        ct = cos_t[posc].copy()
        st_ = sin_t[posc].copy()
        ct[pos < 0] = 0.0
        st_[pos < 0] = 0.0
        idx = np.arange(P) % 32
        d["cos128"] = np.ascontiguousarray(ct[:, idx].T)
        d["sin128"] = np.ascontiguousarray(st_[:, idx].T)
        d["qcos128"] = np.ascontiguousarray(ct[128:, idx].T * np.float32(0.125))
        d["qsin128"] = np.ascontiguousarray(st_[128:, idx].T * np.float32(0.125))
        d["maskPD0"] = maskPD0 if j == 0 else sh["maskPD"]
        cores.append(d)
    return sh, cores


class _Res:
    __slots__ = ("name", "w", "readers", "strict")

    def __init__(self, name):
        self.name = name
        self.w = None
        self.readers = {}
        self.strict = False


class Sched:
    ENGS = ("sync", "vector", "tensor", "scalar", "gpsimd")

    def __init__(self, nc, S):
        self.nc = nc
        self.prog = {e: [] for e in self.ENGS}
        self.trk = {e: [S(nc.semaphore("m_" + e)), 0] for e in self.ENGS}
        self.dma = {}
        self.S = S
        self.res = {}
        self.wm = {}
        self.floor = {}
        self.nsteps = 0
        self.nwaits = 0
        self.nfence = 0

    def r(self, name):
        if name not in self.res:
            self.res[name] = _Res(name)
        return self.res[name]

    def _sem(self, semname):
        if semname.startswith("d:"):
            return self.dma[semname[2:]][0]
        return self.trk[semname][0]

    def step(self, eng, reads, writes, fn, dma=None):
        reads = [self.r(n) for n in reads]
        writes = [self.r(n) for n in writes]
        waits = {}

        def need(dep, strict=False):
            if dep is None:
                return
            deng, dsem, dval = dep
            if deng == eng and not dsem.startswith("d:") and not strict:
                return
            key = (eng, dsem)
            if self.wm.get(key, -1) >= dval:
                return
            if waits.get(dsem, -1) < dval:
                waits[dsem] = dval

        for fsem, fval in self.floor.items():
            need(("_fence", fsem, fval))
        for rr in reads:
            need(rr.w, rr.strict)
        for ww in writes:
            need(ww.w, ww.strict)
            for dep in ww.readers.values():
                need(dep, ww.strict)
        if dma is not None:
            if dma not in self.dma:
                self.dma[dma] = [self.S(self.nc.semaphore("d_" + dma)), 0]
            t = self.dma[dma]
            t[1] += 16
            semname, val, inc = "d:" + dma, t[1], 16
        else:
            t = self.trk[eng]
            t[1] += 1
            semname, val, inc = eng, t[1], 1
        for ww in writes:
            ww.w = (eng, semname, val)
            ww.readers = {}
        for rr in reads:
            rr.readers[eng] = (eng, semname, val)
        for dsem, dval in waits.items():
            self.wm[(eng, dsem)] = dval
        wl = sorted(waits.items())
        self.nsteps += 1
        self.nwaits += len(wl)
        sem_h = self._sem(semname)

        def go(engobj, wl=wl, fn=fn, sem_h=sem_h, inc=inc):
            for dsem, dval in wl:
                engobj.wait_ge(self._sem(dsem), dval)
            fn(engobj).then_inc(sem_h, inc)

        self.prog[eng].append(go)

    def strict(self, *names):
        for n in names:
            self.r(n).strict = True

    def fence(self):
        self.nfence += 1
        allres = list(self.res.keys())
        marks = {}
        for e in self.ENGS:
            self.step(e, allres, [f"_f{self.nfence}_{e}"],
                      lambda x: x.nop())
            marks[e] = self.trk[e][1]
        self.floor = dict(marks)

    def emit(self):
        with self.nc.Block() as blk:
            @blk.sync
            def _(e):
                for g in self.prog["sync"]:
                    g(e)

            @blk.vector
            def _(e):
                for g in self.prog["vector"]:
                    g(e)

            @blk.tensor
            def _(e):
                for g in self.prog["tensor"]:
                    g(e)

            @blk.scalar
            def _(e):
                for g in self.prog["scalar"]:
                    g(e)

            @blk.gpsimd
            def _(e):
                for g in self.prog["gpsimd"]:
                    g(e)


def build_nc():
    nc = bass.Bass("TRN2", target_bir_lowering=False, debug=False,
                   detect_race_conditions=False)

    def inp(name, shape, dt=F32):
        return nc.dram_tensor(name, shape, dt, kind="ExternalInput").ap()

    x_fm = inp("x_fm", [DMODEL, TOKW])
    qkv_wT_d = inp("qkv_wT", [DMODEL, 3072])
    qkv_bT_d = inp("qkv_bT", [P, 20])
    vbias_d = inp("vbias", [1, 512])
    out_wT_d = inp("out_wT", [DMODEL, DMODEL])
    out_bC_d = inp("out_bC", [P, 16])
    sinks_d = inp("sinks128", [P, 32])
    w1T_d = inp("w1T", [DMODEL, 8192])
    w2T_d = inp("w2T", [8192, DMODEL])
    cos_d = inp("cos128", [P, TOKW])
    sin_d = inp("sin128", [P, TOKW])
    qcos_d = inp("qcos128", [P, NTOK])
    qsin_d = inp("qsin128", [P, NTOK])
    ident_d = inp("ident", [P, P], BF16)
    onesc_d = inp("ones_col", [P, 1])
    onesr_d = inp("ones_row", [1, P])
    sperm_d = inp("sperm", [P, P])
    maskPD_d = inp("maskPD", [P, 256], BF16)
    maskPD0_d = inp("maskPD0", [P, 256], BF16)
    gp_d = inp("gp", [P, 512], BF16)
    ep_d = inp("ep", [32, 2048])
    epb_d = inp("epb", [32, 2048], BF16)
    epb2_d = inp("epb2", [64, 2048], BF16)

    o_fm = nc.dram_tensor("o_fm", [DMODEL, NTOK], F32, kind="ExternalOutput").ap()
    dbg = {}
    if DEBUG:
        for nm, shp, dt in [("xn", [P, NCH * TOKW], BF16),
                            ("kb", [P, NKV * TOKW], BF16),
                            ("vt", [P, 9 * 512], BF16),
                            ("attn0", [P, NCH * 512], BF16),
                            ("attn1", [P, NCH * 512], BF16),
                            ("acc1", [P, NCH * NTOK], F32),
                            ("acc2", [P, NCH * NTOK], F32),
                            ("rstd", [P, NTOK], F32),
                            ("sc0", [P, 512], F32), ("sc1", [P, 512], F32),
                            ("smal", [P, 20], F32),
                            ("praw", [P, 1024], BF16),
                            ("ptsb", [P, 1024], BF16)]:
            dbg[nm] = nc.dram_tensor("dbg_" + nm, shp, dt,
                                     kind="ExternalOutput").ap()
    u_scr = nc.dram_tensor("u_scr", [8192, NTOK], BF16, kind="Internal").ap()
    w2q_scr = nc.dram_tensor("w2q_scr", [8192, DMODEL], BF16, kind="Internal").ap()

    root = ExitStack()
    S = root.enter_context
    sch = Sched(nc, S)
    st = sch.step
    sch.strict("ms_acc", "scal_t", "rs_t",
               "rs_hl", "rs_lo", "mean", "work", "mrs_r", "rstd_r")
    for _pp in range(2):
        sch.strict(f"rowsum{_pp}", f"rden{_pp}", f"den_t{_pp}",
                   f"m_neg{_pp}", f"es_t{_pp}")

    xn = S(nc.sbuf_tensor("xn", [P, NCH * TOKW], BF16))
    acc = S(nc.sbuf_tensor("acc", [P, NCH * NTOK], F32))
    identt = S(nc.sbuf_tensor("identt", [P, P], BF16))
    ones_col = S(nc.sbuf_tensor("ones_colt", [P, 1], F32))
    ones_row = S(nc.sbuf_tensor("ones_rowt", [1, P], F32))
    spermt = S(nc.sbuf_tensor("spermt", [P, P], F32))
    maskPD = S(nc.sbuf_tensor("maskPDt", [P, 256], BF16))
    maskPD0 = S(nc.sbuf_tensor("maskPD0t", [P, 256], BF16))
    qkvbT = S(nc.sbuf_tensor("qkvbT", [P, 20], F32))
    outbC = S(nc.sbuf_tensor("outbC", [P, 16], F32))
    vbias_t = S(nc.sbuf_tensor("vbias_t", [1, 512], F32))
    sinks_t = S(nc.sbuf_tensor("sinks_t", [P, 32], F32))
    epsln = S(nc.sbuf_tensor("epsln", [1, 1], F32))
    epsrms = S(nc.sbuf_tensor("epsrms", [1, 1], F32))
    rstd128 = S(nc.sbuf_tensor("rstd128", [P, NTOK], F32))

    NSL = ((0, 512), (512, 1024), (1024, TOKW))
    xnr = [f"xn{c}" for c in range(NCH)]

    for dst, src, nm in [(identt, ident_d, "identt"), (ones_col, onesc_d, "ones_col"),
                         (ones_row, onesr_d, "ones_row"), (spermt, sperm_d, "spermt"),
                         (maskPD, maskPD_d, "maskPD"), (maskPD0, maskPD0_d, "maskPD0"),
                         (qkvbT, qkv_bT_d, "qkvbT"), (outbC, out_bC_d, "outbC"),
                         (vbias_t, vbias_d, "vbias"), (sinks_t, sinks_d, "sinks")]:
        st("sync", [], [nm],
           (lambda e, dst=dst, src=src: e.dma_start(out=dst[:], in_=src)),
           dma="tab_" + nm)
    onesb = S(nc.sbuf_tensor("onesb", [P, 1], BF16))
    st("vector", [], ["onesb"], lambda e: e.memset(onesb[:], 1.0))
    st("vector", [], ["epsln", "epsrms"],
       lambda e: (e.memset(epsln[:], LN_EPS), e.memset(epsrms[:], RMS_EPS))[-1])

    ph1 = ExitStack()
    Sp = ph1.enter_context
    xst = [Sp(nc.sbuf_tensor(f"xst{i}", [P, TOKW], F32)) for i in range(3)]
    xsq = Sp(nc.sbuf_tensor("xsq", [P, TOKW], F32))
    statb = Sp(nc.sbuf_tensor("statb", [1, 3 * TOKW], F32))
    mean = statb[:, 0:TOKW]
    work = statb[:, TOKW:2 * TOKW]
    mrs_r = statb[:, 2 * TOKW:3 * TOKW]
    m2 = var = lnv = rs_r = work

    with ExitStack() as ph1a:
        ps_sum = ph1a.enter_context(nc.psum_tensor("ps_sum", [1, TOKW], F32))
        ps_sq = ph1a.enter_context(nc.psum_tensor("ps_sq", [1, TOKW], F32))
        for c in range(NCH):
            bn = f"xst{c % 3}"
            st("sync", [], [bn],
               (lambda e, c=c: e.dma_start(out=xst[c % 3][:],
                                           in_=x_fm[c * P:(c + 1) * P, :])),
               dma=bn)
            st("scalar", [bn], ["xsq"],
               (lambda e, c=c: e.activation(out=xsq[:], in_=xst[c % 3][:],
                                            func=AF.Square)))

            def mm_sums(e, c=c):
                for (lo, hi) in NSL:
                    e.matmul(out=ps_sum[:, lo:hi], lhsT=ones_col[:],
                             rhs=xst[c % 3][:, lo:hi], start=(c == 0),
                             stop=(c == NCH - 1))
                    i = e.matmul(out=ps_sq[:, lo:hi], lhsT=ones_col[:],
                                 rhs=xsq[:, lo:hi], start=(c == 0),
                                 stop=(c == NCH - 1))
                return i
            st("tensor", [bn, "xsq", "ones_col"], ["ps_sum", "ps_sq"], mm_sums)
            st("vector", [bn], [f"acc{c}"],
               (lambda e, c=c: e.tensor_copy(out=acc[:, c * NTOK:(c + 1) * NTOK],
                                             in_=xst[c % 3][:, HALO:])))
        st("vector", ["ps_sum", "ps_sq"], ["mean", "work", "mrs_r"],
           lambda e: (e.tensor_scalar(out=mean, in0=ps_sum[:], scalar1=1.0 / DMODEL,
                                      scalar2=None, op0=ALU.mult),
                      e.tensor_scalar(out=work, in0=ps_sq[:], scalar1=1.0 / DMODEL,
                                      scalar2=None, op0=ALU.mult),
                      e.scalar_tensor_tensor(out=mrs_r, in0=mean, scalar=-1.0,
                                             in1=mean, op0=ALU.mult, op1=ALU.mult),
                      e.tensor_add(out=work, in0=work, in1=mrs_r))[-1])
        st("scalar", ["work", "epsln"], ["work"],
           lambda e: (e.activation(out=lnv, in_=var, func=AF.Ln, bias=epsln[:]),
                      e.activation(out=rs_r, in_=lnv, func=AF.Exp, scale=-0.5))[-1])
        st("vector", ["work", "mean"], ["mrs_r"],
           lambda e: e.tensor_mul(out=mrs_r, in0=mean, in1=rs_r))

    sch.fence()
    ps_rs1 = Sp(nc.psum_tensor("ps_rs1", [P, TOKW], F32))
    ps_mrs = Sp(nc.psum_tensor("ps_mrs", [P, TOKW], F32))

    def mm_bcast(e):
        for (lo, hi) in NSL:
            e.matmul(out=ps_rs1[:, lo:hi], lhsT=ones_row[:], rhs=rs_r[:, lo:hi],
                     start=True, stop=True)
            i = e.matmul(out=ps_mrs[:, lo:hi], lhsT=ones_row[:],
                         rhs=mrs_r[:, lo:hi], start=True, stop=True)
        return i
    st("tensor", ["work", "mrs_r", "ones_row"], ["ps_rs1", "ps_mrs"], mm_bcast)

    for c in range(NCH):
        bn = f"xst{c % 3}"
        st("sync", [], [bn],
           (lambda e, c=c: e.dma_start(out=xst[c % 3][:, 0:P],
                                       in_=x_fm[c * P:(c + 1) * P, 0:P])),
           dma=bn)
        st("vector", [bn, "ps_rs1", "ps_mrs", f"acc{c}"], [f"xn{c}", "xsq"],
           (lambda e, c=c: (
               e.tensor_mul(out=xsq[:, 0:P], in0=xst[c % 3][:, 0:P],
                            in1=ps_rs1[:, 0:P]),
               e.tensor_sub(out=xn[:, c * TOKW:c * TOKW + P],
                            in0=xsq[:, 0:P], in1=ps_mrs[:, 0:P]),
               e.tensor_mul(out=xsq[:, P:TOKW],
                            in0=acc[:, c * NTOK:(c + 1) * NTOK],
                            in1=ps_rs1[:, P:TOKW]),
               e.tensor_sub(out=xn[:, c * TOKW + P:(c + 1) * TOKW],
                            in0=xsq[:, P:TOKW],
                            in1=ps_mrs[:, P:TOKW]))[-1]))
    if DEBUG:
        st("gpsimd", [f"acc{m}" for m in range(16)], [],
           lambda e: e.dma_start(out=dbg["acc1"], in_=acc[:]), dma="dbg")
        st("gpsimd", xnr, [],
           lambda e: e.dma_start(out=dbg["xn"], in_=xn[:]), dma="dbg")
    ph1.close()
    sch.fence()

    qkv_r = qkv_wT_d.rearrange("(c p) q -> c p q", p=P)
    out_r = out_wT_d.rearrange("(c p) q -> c p q", p=P)
    w1_r = w1T_d.rearrange("(c p) q -> c p q", p=P)
    w2_r = w2T_d.rearrange("(c p) q -> c p q", p=P)

    def load_wslab(src_r, colbase, width, half, wt):
        for cp in range(4):
            st("gpsimd", [], [f"wbf{half}"],
               (lambda e, cp=cp, colbase=colbase, width=width, src_r=src_r,
                        w=wt[half]:
                e.dma_start(
                    out=w[:, 4 * cp * width:(4 * cp + 4) * width].rearrange(
                        "p (c q) -> p c q", c=4),
                    in_=src_r[4 * cp:4 * cp + 4, :, colbase:colbase + width]
                        .rearrange("c p q -> p c q"))),
               dma=f"wld{half}")

    if PHASES < 2:
        st("gpsimd", [f"acc{m}" for m in range(16)] + xnr, ["odram"],
           lambda e: e.dma_start(
               out=o_fm.rearrange("(c p) q -> c p q", p=P).rearrange("c p q -> p c q"),
               in_=acc[:].rearrange("p (c q) -> p c q", c=NCH)),
           dma="ost")
        fv = sch.dma["ost"][1]
        sch.prog["gpsimd"].append(
            lambda e, fv=fv: e.wait_ge(sch.dma["ost"][0], fv))
        sch.emit()
        root.close()
        return nc, sch

    ph23 = ExitStack()
    S23 = ph23.enter_context
    k_both = S23(nc.sbuf_tensor("k_both", [P, NKV * TOKW], BF16))
    v_t = S23(nc.sbuf_tensor("v_t", [P, 9 * 512], BF16))
    qcost = S23(nc.sbuf_tensor("qcost", [P, NTOK], F32))
    qsint = S23(nc.sbuf_tensor("qsint", [P, NTOK], F32))
    st("sync", [], ["qcost"],
       lambda e: e.dma_start(out=qcost[:], in_=qcos_d), dma="tab_qcost")
    st("sync", [], ["qsint"],
       lambda e: e.dma_start(out=qsint[:], in_=qsin_d), dma="tab_qsint")
    ph2 = ExitStack()
    Sp = ph2.enter_context
    cost = Sp(nc.sbuf_tensor("cost", [P, TOKW], F32))
    sint = Sp(nc.sbuf_tensor("sint", [P, TOKW], F32))
    st("sync", [], ["cost"],
       lambda e: e.dma_start(out=cost[:], in_=cos_d), dma="tab_cost")
    st("sync", [], ["sint"],
       lambda e: e.dma_start(out=sint[:], in_=sin_d), dma="tab_sint")
    kf32 = Sp(nc.sbuf_tensor("kf32", [P, TOKW], F32))
    t1s = Sp(nc.sbuf_tensor("t1s", [P, TOKW], F32))
    krtmp = Sp(nc.sbuf_tensor("krtmp", [P, TOKW], BF16))
    ps_k = Sp(nc.psum_tensor("ps_k", [P, TOKW], F32))
    ps_rot = Sp(nc.psum_tensor("ps_rot", [P, TOKW], F32))
    ps_v = Sp(nc.psum_tensor("ps_v", [P, 512], F32))
    wbf2 = [Sp(nc.sbuf_tensor(f"wbf2_{i}", [P, 8192], BF16)) for i in range(2)]

    load_wslab(qkv_r, 2048, 512, 0, wbf2)
    load_wslab(qkv_r, 2560, 512, 1, wbf2)

    for m in range(4):
        def mm_k(e, m=m, w=None):
            w = wbf2[0]
            for c in range(NCH):
                for (lo, hi) in NSL:
                    i = e.matmul(
                        out=ps_k[:, lo:hi],
                        lhsT=w[:, c * 512 + m * P:c * 512 + (m + 1) * P],
                        rhs=xn[:, c * TOKW + lo:c * TOKW + hi],
                        start=(c == 0), stop=(c == NCH - 1))
            return i
        st("tensor", ["wbf0"] + xnr, ["ps_k"], mm_k)
        st("scalar", ["ps_k", "qkvbT"], ["kf32"],
           (lambda e, m=m: e.activation(out=kf32[:], in_=ps_k[:], func=AF.Identity,
                                        bias=qkvbT[:, 16 + m:17 + m])))

        def mm_rot(e):
            for (lo, hi) in NSL:
                i = e.matmul(out=ps_rot[:, lo:hi], lhsT=spermt[:],
                             rhs=kf32[:, lo:hi], start=True, stop=True)
            return i
        st("tensor", ["kf32", "spermt"], ["ps_rot"], mm_rot)
        st("vector", ["kf32", "ps_rot", "cost", "sint"], ["krtmp", "t1s", "kf32"],
           (lambda e, m=m: (
               e.tensor_mul(out=t1s[:], in0=kf32[:], in1=cost[:]),
               e.tensor_mul(out=kf32[:], in0=ps_rot[:], in1=sint[:]),
               e.tensor_add(out=krtmp[:], in0=t1s[:], in1=kf32[:]))[-1]))
        for gg in range(2):
            g = 2 * m + gg
            bsrc = 64 * gg
            st("vector", ["krtmp"], [f"kb{g}"],
               (lambda e, g=g, bsrc=bsrc: (
                   e.tensor_copy(out=k_both[0:64, g * TOKW:(g + 1) * TOKW],
                                 in_=krtmp[bsrc:bsrc + 64, :]),
                   e.tensor_copy(out=k_both[64:128, g * TOKW:(g + 1) * TOKW],
                                 in_=krtmp[bsrc:bsrc + 64, :]))[-1]))
    for tt in range(9):
        def mm_v(e, tt=tt):
            for c in range(NCH):
                e.matmul(out=ps_v[:],
                         lhsT=xn[:, c * TOKW + tt * P:c * TOKW + (tt + 1) * P],
                         rhs=wbf2[1][:, c * 512:(c + 1) * 512],
                         start=(c == 0), stop=False)
            return e.matmul(out=ps_v[:], lhsT=ones_row[:], rhs=vbias_t[:],
                            start=False, stop=True)
        st("tensor", ["wbf1", "vbias", "ones_row"] + xnr, ["ps_v"], mm_v)
        st("scalar", ["ps_v"], [f"vt{tt}"],
           (lambda e, tt=tt: e.copy(out=v_t[:, tt * 512:(tt + 1) * 512],
                                    in_=ps_v[:])))
    if DEBUG:
        st("gpsimd", [f"kb{g}" for g in range(NKV)], [],
           lambda e: e.dma_start(out=dbg["kb"], in_=k_both[:]), dma="dbg")
        st("gpsimd", [f"vt{t}" for t in range(9)], [],
           lambda e: e.dma_start(out=dbg["vt"], in_=v_t[:]), dma="dbg")
    ph2.close()
    sch.fence()

    if PHASES < 3:
        ph23.close()
        st("gpsimd", [f"acc{m}" for m in range(16)] + [f"kb{g}" for g in range(8)] + [f"vt{t}" for t in range(9)], ["odram"],
           lambda e: e.dma_start(
               out=o_fm.rearrange("(c p) q -> c p q", p=P).rearrange("c p q -> p c q"),
               in_=acc[:].rearrange("p (c q) -> p c q", c=NCH)),
           dma="ost")
        fv = sch.dma["ost"][1]
        sch.prog["gpsimd"].append(
            lambda e, fv=fv: e.wait_ge(sch.dma["ost"][0], fv))
        sch.emit()
        root.close()
        return nc, sch

    ph3 = ExitStack()
    Sp = ph3.enter_context
    qf32 = Sp(nc.sbuf_tensor("qf32", [P, 512], F32))
    t1q = Sp(nc.sbuf_tensor("t1q", [P, 512], F32))
    qr = Sp(nc.sbuf_tensor("qr", [P, 1024], BF16))
    qz = Sp(nc.sbuf_tensor("qz", [P, 4096], BF16))
    attn_h = Sp(nc.sbuf_tensor("attn_h", [P, NCH * 512], BF16))
    praw = Sp(nc.sbuf_tensor("praw", [P, 2048], BF16))
    pt_sb = Sp(nc.sbuf_tensor("pt_sb", [P, 1024], BF16))
    smal = Sp(nc.sbuf_tensor("smal", [P, 40], F32))
    ps_q = Sp(nc.psum_tensor("ps_q", [P, 512], F32))
    ps_rq = Sp(nc.psum_tensor("ps_rq", [P, 512], F32))
    ps_s = Sp(nc.psum_tensor("ps_s", [P, 2048], F32))
    ps_t = Sp(nc.psum_tensor("ps_t", [P, 1024], BF16))
    ps_av = Sp(nc.psum_tensor("ps_av", [P, 256], F32))
    ps_o = ps_rq
    wbf3 = [Sp(nc.sbuf_tensor(f"wbf3_{i}", [P, 4096], BF16)) for i in range(2)]

    st("vector", [], ["praw0", "praw1"], lambda e: e.memset(praw[:], 0.0))
    st("vector", [], ["qz0", "qz1"], lambda e: e.memset(qz[:], 0.0))

    pend = []

    for half in range(2):
        hlo = HALO + 512 * half
        def qprep_pieces(g, hlo=hlo, half=half):
            gp_ = g % 2
            wq = g % 2
            pcs = []
            for mq in range(2):
                def p_mm(mq=mq, wq=wq, hlo=hlo, g=g):
                    def mm_q(e, mq=mq, hlo=hlo, wq=wq):
                        for c in range(NCH):
                            i = e.matmul(
                                out=ps_q[:],
                                lhsT=wbf3[wq][:, c * 256 + mq * P:
                                              c * 256 + (mq + 1) * P],
                                rhs=xn[:, c * TOKW + hlo:c * TOKW + hlo + 512],
                                start=(c == 0), stop=(c == NCH - 1))
                        return i
                    st("tensor", [f"wbf{wq}"] + xnr, ["ps_q"], mm_q)
                    st("scalar", ["ps_q", "qkvbT"], ["qf32"],
                       (lambda e, g=g, mq=mq: e.activation(
                           out=qf32[:], in_=ps_q[:],
                           func=AF.Identity,
                           bias=qkvbT[:, 2 * g + mq:2 * g + mq + 1])))

                def p_rope(mq=mq, half=half, gp_=gp_, last=(mq == 1)):
                    st("tensor", ["qf32", "spermt"], ["ps_rq"],
                       lambda e: e.matmul(out=ps_rq[:], lhsT=spermt[:],
                                          rhs=qf32[:], start=True, stop=True))
                    st("vector", ["qf32", "ps_rq", "qcost", "qsint"],
                       [f"qr{mq}", "t1q", "qf32"],
                       (lambda e, mq=mq, half=half: (
                           e.tensor_mul(out=t1q[:], in0=qf32[:],
                                        in1=qcost[:, half * 512:(half + 1) * 512]),
                           e.tensor_mul(out=qf32[:], in0=ps_rq[:],
                                        in1=qsint[:, half * 512:(half + 1) * 512]),
                           e.tensor_add(out=qr[:, mq * 512:(mq + 1) * 512],
                                        in0=t1q[:], in1=qf32[:]))[-1]))
                    if last:
                        def do_qz(e, gp_=gp_):
                            for h in range(4):
                                hb = 64 * (h % 2)
                                i = e.tensor_copy(
                                    out=qz[hb:hb + 64,
                                           2048 * gp_ + 512 * h:
                                           2048 * gp_ + 512 * (h + 1)],
                                    in_=qr[hb:hb + 64,
                                           (h // 2) * 512:(h // 2 + 1) * 512])
                            return i
                        st("vector", ["qr0", "qr1"], [f"qz{gp_}"], do_qz)
                pcs.append(p_mm)
                pcs.append(p_rope)
            return pcs

        qpp = []
        for g in range(NKV):
            gp_ = g % 2
            if g == 0:
                load_wslab(qkv_r, 0, 256, 0, wbf3)
                for t_ in qprep_pieces(0):
                    t_()
            if g + 1 < NKV:
                load_wslab(qkv_r, 256 * (g + 1), 256, (g + 1) % 2, wbf3)
                qpp = qprep_pieces(g + 1)
            else:
                qpp = []
            for qt in range(4):
                if PH3_LEVEL < 1:
                    continue
                aqt = 4 * half + qt
                par = (4 * g + qt) % 2
                so = 1024 * par
                sm0 = 20 * par
                m_neg = smal[:, sm0:sm0 + 4]
                rowsum = smal[:, sm0 + 4:sm0 + 8]
                es_t = smal[:, sm0 + 8:sm0 + 12]
                den_t = smal[:, sm0 + 12:sm0 + 16]
                rden = smal[:, sm0 + 16:sm0 + 20]

                def mm_s(e, g=g, qt=qt, aqt=aqt, gp_=gp_, so=so):
                    for h in range(4):
                        kt = k_both[:, g * TOKW:(g + 1) * TOKW]
                        i = e.matmul(
                            out=ps_s[:, so + h * 256:so + (h + 1) * 256],
                            lhsT=qz[:, 2048 * gp_ + 512 * h + qt * P:
                                    2048 * gp_ + 512 * h + (qt + 1) * P],
                            rhs=kt[:, P * aqt:P * aqt + 256],
                            start=True, stop=True)
                    return i
                st("tensor", [f"qz{gp_}", f"kb{g}"], [f"ps_s{par}"], mm_s)
                if PH3_LEVEL < 2:
                    continue
                if pend:
                    pend[0]()
                st("vector", [f"ps_s{par}"], [f"m_neg{par}"],
                   (lambda e, so=so, m_neg=m_neg: e.tensor_reduce(
                       out=m_neg,
                       in_=ps_s[:, so:so + 1024].rearrange(
                           "p (h k) -> p h k", k=256),
                       axis=AX.X, op=ALU.max, negate=True)))
                for th_ in pend[1:]:
                    th_()
                pend = []

                def do_exp(e, so=so, m_neg=m_neg):
                    for h in range(4):
                        i = e.activation(
                            out=praw[:, so + h * 256:so + (h + 1) * 256],
                            in_=ps_s[:, so + h * 256:so + (h + 1) * 256],
                            func=AF.Exp, bias=m_neg[:, h:h + 1])
                    return i
                st("scalar", [f"ps_s{par}", f"m_neg{par}"], [f"praw{par}"], do_exp)

                def do_es(e, g=g, m_neg=m_neg, es_t=es_t):
                    for h in range(4):
                        i = e.activation(out=es_t[:, h:h + 1],
                                         in_=m_neg[:, h:h + 1], func=AF.Exp,
                                         bias=sinks_t[:, 4 * g + h:4 * g + h + 1])
                    return i
                st("scalar", [f"m_neg{par}", "sinks"], [f"es_t{par}"], do_es)
                mkn = "maskPD0" if aqt == 0 else "maskPD"
                mk = maskPD0 if aqt == 0 else maskPD

                def do_mask(e, mk=mk, so=so, rowsum=rowsum):
                    for h in range(4):
                        i = e.scalar_tensor_tensor(
                            out=praw[:, so + h * 256:so + (h + 1) * 256],
                            in0=praw[:, so + h * 256:so + (h + 1) * 256],
                            scalar=1.0,
                            in1=mk[:], op0=ALU.mult, op1=ALU.mult,
                            accum_out=rowsum[:, h:h + 1])
                    return i
                st("vector", [f"praw{par}", mkn],
                   [f"praw{par}", f"rowsum{par}"], do_mask)
                st("vector", [f"rowsum{par}", f"es_t{par}"], [f"den_t{par}"],
                   (lambda e, den_t=den_t, rowsum=rowsum, es_t=es_t:
                    e.tensor_add(out=den_t, in0=rowsum, in1=es_t)))
                st("vector", [f"den_t{par}"], [f"rden{par}"],
                   (lambda e, rden=rden, den_t=den_t:
                    e.reciprocal(out=rden, in_=den_t)))

                def do_norm(e, so=so, rden=rden):
                    for h in range(4):
                        i = e.tensor_scalar(
                            out=praw[:, so + h * 256:so + (h + 1) * 256],
                            in0=praw[:, so + h * 256:so + (h + 1) * 256],
                            scalar1=rden[:, h:h + 1],
                            scalar2=None, op0=ALU.mult)
                    return i
                st("vector", [f"rden{par}", f"praw{par}"], [f"praw{par}"],
                   do_norm)
                if PH3_LEVEL < 3:
                    continue

                def do_pt(e, so=so):
                    for h in range(4):
                        for k2 in range(2):
                            i = e.transpose(
                                out=ps_t[:, k2 * 512 + h * P:k2 * 512 + (h + 1) * P],
                                in_=praw[:, so + h * 256 + k2 * P:
                                         so + h * 256 + (k2 + 1) * P],
                                identity=identt[:])
                    return i

                def do_av(e, g=g, aqt=aqt):
                    for h in (0, 2, 1, 3):
                        ob = 64 * (h % 2)
                        oc = P * (h // 2)
                        e.matmul(out=ps_av[ob:ob + 64, oc:oc + P],
                                 lhsT=v_t[:, aqt * 512 + 64 * g:aqt * 512 + 64 * g + 64],
                                 rhs=pt_sb[:, h * P:(h + 1) * P],
                                 start=True, stop=False)
                        i = e.matmul(
                            out=ps_av[ob:ob + 64, oc:oc + P],
                            lhsT=v_t[:, (aqt + 1) * 512 + 64 * g:
                                     (aqt + 1) * 512 + 64 * g + 64],
                            rhs=pt_sb[:, 512 + h * P:512 + (h + 1) * P],
                            start=False, stop=True)
                    return i

                def do_avev(e, g=g, qt=qt):
                    for hh in range(2):
                        i = e.copy(
                            out=attn_h[:, (2 * g + hh) * 512 + qt * P:
                                       (2 * g + hh) * 512 + (qt + 1) * P],
                            in_=ps_av[:, hh * P:(hh + 1) * P])
                    return i
                pend = [
                    (lambda par=par, do_pt=do_pt: st(
                        "tensor", [f"praw{par}", "identt"], ["ps_t"], do_pt)),
                    (lambda: st("vector", ["ps_t"], ["pt_sb"],
                                lambda e: e.tensor_copy(out=pt_sb[:],
                                                        in_=ps_t[:]))),
                    (lambda aqt=aqt, do_av=do_av: st(
                        "tensor", ["pt_sb", f"vt{aqt}", f"vt{aqt + 1}"],
                        ["ps_av"], do_av)),
                    (lambda g=g, do_avev=do_avev: st(
                        "scalar", ["ps_av"],
                        [f"attnh{2 * g}", f"attnh{2 * g + 1}"], do_avev)),
                ]
                if qpp:
                    qpp.pop(0)()
            while qpp:
                qpp.pop(0)()
        if PH3_LEVEL < 5:
            continue
        for th_ in pend:
            th_()
        pend = []
        if DEBUG:
            st("gpsimd", [f"attnh{i}" for i in range(16)], [],
               (lambda e, half=half: e.dma_start(out=dbg["attn" + str(half)],
                                                 in_=attn_h[:])), dma="dbg")
        ares = [f"attnh{i}" for i in range(16)]
        for sl in range(8):
            wo = sl % 2
            load_wslab(out_r, 256 * sl, 256, wo, wbf3)
            for dm in (2 * sl, 2 * sl + 1):
                pst = ps_q if dm % 2 else ps_rq
                psn = "ps_q" if dm % 2 else "ps_rq"

                def mm_o(e, dm=dm, pst=pst, wo=wo):
                    for c in range(NCH):
                        i = e.matmul(
                            out=pst[:],
                            lhsT=wbf3[wo][:, c * 256 + (dm % 2) * P:
                                          c * 256 + (dm % 2 + 1) * P],
                            rhs=attn_h[:, c * 512:(c + 1) * 512],
                            start=(c == 0), stop=(c == NCH - 1))
                    return i
                st("tensor", [f"wbf{wo}"] + ares, [psn], mm_o)
                st("vector", [psn, f"acc{dm}"], [f"acc{dm}"],
                   (lambda e, dm=dm, half=half, pst=pst: e.tensor_add(
                       out=acc[:, dm * NTOK + 512 * half:
                               dm * NTOK + 512 * (half + 1)],
                       in0=acc[:, dm * NTOK + 512 * half:
                               dm * NTOK + 512 * (half + 1)],
                       in1=pst[:])))
    for dm in range(16):
        st("vector", [f"acc{dm}", "outbC"], [f"acc{dm}"],
           (lambda e, dm=dm: e.tensor_scalar(
               out=acc[:, dm * NTOK:(dm + 1) * NTOK],
               in0=acc[:, dm * NTOK:(dm + 1) * NTOK],
               scalar1=outbC[:, dm:dm + 1], scalar2=None, op0=ALU.add)))
    if DEBUG:
        st("gpsimd", [f"acc{m}" for m in range(16)], [],
           lambda e: e.dma_start(out=dbg["acc2"], in_=acc[:]), dma="dbg")
    ph3.close()
    ph23.close()
    sch.fence()

    if PHASES < 4:
        st("gpsimd", [f"acc{m}" for m in range(16)], ["odram"],
           lambda e: e.dma_start(
               out=o_fm.rearrange("(c p) q -> c p q", p=P).rearrange("c p q -> p c q"),
               in_=acc[:].rearrange("p (c q) -> p c q", c=NCH)),
           dma="ost")
        fv = sch.dma["ost"][1]
        sch.prog["gpsimd"].append(
            lambda e, fv=fv: e.wait_ge(sch.dma["ost"][0], fv))
        sch.emit()
        root.close()
        return nc, sch

    ph4 = ExitStack()
    Sp = ph4.enter_context
    wbf4 = [Sp(nc.sbuf_tensor(f"wbf4_{i}", [P, 8192], BF16)) for i in range(2)]
    gpt = Sp(nc.sbuf_tensor("gpt", [P, 512], BF16))
    epbt2 = Sp(nc.sbuf_tensor("epbt2", [64, 2048], BF16))
    epbt = epbt2[0:32, :]
    qabs = [Sp(nc.sbuf_tensor(f"qabs{i}", [P, 512], BF16)) for i in range(2)]
    scal_t = Sp(nc.sbuf_tensor("scal_t", [32, 512], BF16))
    rs_t = Sp(nc.sbuf_tensor("rs_t", [32, 512], F32))
    qp_t = Sp(nc.sbuf_tensor("qp_t", [P, 512], BF16))
    rs_hl = Sp(nc.sbuf_tensor("rs_hl", [64, 512], BF16))
    rs_lo = Sp(nc.sbuf_tensor("rs_lo", [32, 512], BF16))
    zp_t = Sp(nc.sbuf_tensor("zp_t", [P, 512], BF16))
    zn_t = Sp(nc.sbuf_tensor("zn_t", [P, 512], BF16))
    u_sb = Sp(nc.sbuf_tensor("u_sb", [P, 4 * 1024], BF16))
    u2_sb = Sp(nc.sbuf_tensor("u2_sb", [P, 1024], BF16))
    w2ob = Sp(nc.sbuf_tensor("w2ob", [P, 16 * 512], BF16))
    rstd_r = Sp(nc.sbuf_tensor("rstd_r", [1, NTOK], F32))
    ps_g4 = Sp(nc.psum_tensor("ps_g4", [32, 512], F32))
    ps_rs4 = Sp(nc.psum_tensor("ps_rs4", [P, 1024], F32))
    ps_sc4 = Sp(nc.psum_tensor("ps_sc4", [P, 1024], F32))
    ps_h = Sp(nc.psum_tensor("ps_h", [P, 1024], F32))
    ps_ms = Sp(nc.psum_tensor("ps_ms", [P, 512], F32))
    ms_acc = Sp(nc.sbuf_tensor("ms_acc", [1, NTOK], F32))
    st("sync", [], ["gpt"],
       lambda e: e.dma_start(out=gpt[:], in_=gp_d), dma="tab_gpt")
    st("sync", [], ["epbt", "epbt2"],
       lambda e: e.dma_start(out=epbt2[:], in_=epb2_d), dma="tab_epbt2")

    def quant_batch(src_r, colbase, chunk0, dst, out_view_fn, bufkey,
                    after_apply=None, width=512, hook=None, hook2=None):
        for cp in range(4):
            rns = sorted({out_view_fn(chunk0 + 4 * cp + j)[1] for j in range(4)})
            st("gpsimd", [], rns,
               (lambda e, cp=cp, chunk0=chunk0, colbase=colbase, src_r=src_r,
                        width=width, dst=dst:
                e.dma_start(
                    out=dst[:, 4 * cp * width:(4 * cp + 4) * width].rearrange(
                        "p (c q) -> p c q", c=4),
                    in_=src_r[chunk0 + 4 * cp:chunk0 + 4 * cp + 4, :,
                              colbase:colbase + width]
                        .rearrange("c p q -> p c q"))),
               dma=f"wq{bufkey}_{cp}")
        for c in range(16):
            cg = chunk0 + c
            ov, orn = out_view_fn(cg)
            if hook is not None and c % 2 == 0:
                hook()
            st("scalar", [orn], [f"qabs{c % 2}"],
               (lambda e, ov=ov, c=c, width=width:
                e.activation(out=qabs[c % 2][:, :width], in_=ov, func=AF.Abs)))
            st("tensor", [f"qabs{c % 2}", "gpt"], ["ps_g4"],
               (lambda e, c=c, width=width: e.matmul(
                   out=ps_g4[:, :width], lhsT=gpt[:, 32 * c:32 * (c + 1)],
                   rhs=qabs[c % 2][:, :width],
                   start=(c == 0), stop=(c == 15))))
            if hook2 is not None and c % 2 == 1:
                hook2()
        st("vector", ["ps_g4"], ["scal_t", "rs_t", "rs_hl", "rs_lo"],
           (lambda e, width=width: (
               e.tensor_scalar(out=scal_t[:, :width], in0=ps_g4[:, :width],
                               scalar1=1.0 / 64, scalar2=1e-8,
                               op0=ALU.mult, op1=ALU.max),
               e.reciprocal(out=rs_t[:, :width], in_=scal_t[:, :width]),
               e.tensor_copy(out=rs_hl[0:32, :width], in_=rs_t[:, :width]),
               e.tensor_sub(out=rs_lo[:, :width], in0=rs_t[:, :width],
                            in1=rs_hl[0:32, :width]),
               e.tensor_copy(out=rs_hl[32:64, :width],
                             in_=rs_lo[:, :width]))[-1]))
        for c in range(16):
            cg = chunk0 + c
            ov, orn = out_view_fn(cg)
            pp = c % 2
            o0 = 512 * pp

            def mm_exp(e, c=c, width=width, o0=o0):
                e.matmul(out=ps_rs4[:, o0:o0 + width],
                         lhsT=epbt2[:, P * c:P * (c + 1)],
                         rhs=rs_hl[:, :width], start=True, stop=True)
                return e.matmul(out=ps_sc4[:, o0:o0 + width],
                                lhsT=epbt2[0:32, P * c:P * (c + 1)],
                                rhs=scal_t[:, :width], start=True, stop=True)
            st("tensor", ["rs_hl", "scal_t", "epbt", "epbt2"],
               [f"ps_rs4{pp}", f"ps_sc4{pp}"], mm_exp)

            def do_apply(e, ov=ov, width=width, o0=o0, pp=pp):
                e.tensor_mul(out=qp_t[:, :width], in0=ov,
                             in1=ps_rs4[:, o0:o0 + width])
                e.tensor_scalar(out=zp_t[:, :width],
                                in0=qp_t[:, :width],
                                scalar1=0.5, scalar2=None, op0=ALU.is_gt)
                i = e.tensor_scalar(out=zn_t[:, :width],
                                    in0=qp_t[:, :width],
                                    scalar1=-0.5, scalar2=None, op0=ALU.is_lt)
                return i
            st("vector", [orn, f"ps_rs4{pp}"],
               [f"qzz{pp}"], do_apply)
            st("vector", [f"qzz{pp}", f"ps_sc4{pp}"], [orn],
               (lambda e, ov=ov, width=width, pp=pp, o0=o0: (
                   e.tensor_sub(out=zp_t[:, :width],
                                in0=zp_t[:, :width],
                                in1=zn_t[:, :width]),
                   e.tensor_mul(out=ov, in0=zp_t[:, :width],
                                in1=ps_sc4[:, o0:o0 + width]))[-1]))
            if after_apply is not None:
                after_apply(cg)

    def w2_view(cg):
        cc = cg % 16
        return w2ob[:, cc * 512:(cc + 1) * 512], f"w2o{cc}"

    def fc_pieces(s):
        half = s % 2
        msp = []
        thunks = []
        for m in range(4):
            for th in range(2):
                def piece_mm(s=s, m=m, th=th, half=half, msp=msp):
                    pb = (2 * m + th) % 2

                    def mm_fc(e, m=m, th=th, pb=pb, half=half):
                        for c in range(NCH):
                            i = e.matmul(
                                out=ps_h[:, pb * 512:(pb + 1) * 512],
                                lhsT=wbf4[half][:, c * 512 + m * P:
                                                c * 512 + (m + 1) * P],
                                rhs=xn[:, c * TOKW + HALO + th * 512:
                                       c * TOKW + HALO + (th + 1) * 512],
                                start=(c == 0), stop=(c == NCH - 1))
                        return i
                    st("tensor",
                       [f"w1c{half}_{cc}" for cc in range(16)] + xnr,
                       [f"ps_h{pb}"], mm_fc)
                    for f in msp:
                        f()
                    msp.clear()

                def piece(s=s, m=m, th=th, half=half, msp=msp):
                    pb = (2 * m + th) % 2
                    st("vector", [f"ps_h{pb}"], [f"u{m}"],
                       (lambda e, m=m, th=th, pb=pb: e.tensor_scalar(
                           out=u_sb[:, m * 1024 + th * 512:m * 1024 + (th + 1) * 512],
                           in0=ps_h[:, pb * 512:(pb + 1) * 512],
                           scalar1=0.0, scalar2=None, op0=ALU.max)))
                    st("scalar", [f"u{m}"], [f"u{m}"],
                       (lambda e, m=m, th=th: e.activation(
                           out=u_sb[:, m * 1024 + th * 512:m * 1024 + (th + 1) * 512],
                           in_=u_sb[:, m * 1024 + th * 512:m * 1024 + (th + 1) * 512],
                           func=AF.Square)))
                    st("scalar", [f"u{m}"], [f"u2{th}"],
                       (lambda e, m=m, th=th: e.activation(
                           out=u2_sb[:, th * 512:(th + 1) * 512],
                           in_=u_sb[:, m * 1024 + th * 512:m * 1024 + (th + 1) * 512],
                           func=AF.Square)))
                    first = (s == 0 and m == 0)
                    last = (s == W1_SLABS - 1 and m == 3)
                    msp.append(lambda th=th, first=first, last=last: st(
                        "tensor", [f"u2{th}", "onesb"], ["ps_ms"],
                        (lambda e, th=th, first=first, last=last: e.matmul(
                            out=ps_ms[64 * th:64 * th + 1, :],
                            lhsT=onesb[:],
                            rhs=u2_sb[:, th * 512:(th + 1) * 512],
                            start=first, stop=last))))
                    if th == 1:
                        hid0 = 512 * s + m * P
                        st("gpsimd", [f"u{m}"], [f"udram{s}_{m}"],
                           (lambda e, hid0=hid0, m=m: e.dma_start(
                               out=u_scr[hid0:hid0 + P, :],
                               in_=u_sb[:, m * 1024:(m + 1) * 1024])),
                           dma=f"ust{m}")
                thunks.append((piece_mm, piece))

        def final(msp=msp):
            for f in msp:
                f()
            msp.clear()
        return thunks, final

    fcq = []
    restq = []
    fin_prev = None

    def hk():
        if fcq:
            mm_t, rest_t = fcq.pop(0)
            mm_t()
            restq.append(rest_t)

    def hk2():
        if restq:
            restq.pop(0)()

    for s in range(W1_SLABS):
        half = s % 2

        def w1_view(c, half=half):
            return wbf4[half][:, c * 512:(c + 1) * 512], f"w1c{half}_{c}"
        quant_batch(w1_r, 512 * s, 0, wbf4[half], w1_view, f"w1{half}",
                    hook=hk, hook2=hk2)
        while fcq or restq:
            hk()
            hk2()
        if fin_prev is not None:
            fin_prev()
        ws = s // 4
        wb_ = s % 4

        def w2_store(cg, ws=ws):
            cc = cg % 16
            st("gpsimd", [f"w2o{cc}"], [f"w2qdram{ws}_{cg}"],
               (lambda e, ws=ws, cg=cg, cc=cc: e.dma_start(
                   out=w2q_scr[P * cg:P * (cg + 1), 512 * ws:512 * (ws + 1)],
                   in_=w2ob[:, cc * 512:(cc + 1) * 512])),
               dma=f"w2st{cc % 4}")
        quant_batch(w2_r, 512 * ws, 16 * wb_, w2ob, w2_view, "w2",
                    after_apply=w2_store)
        fcq, fin_prev = fc_pieces(s)
    while fcq or restq:
        hk()
        hk2()
    if fin_prev is not None:
        fin_prev()
    st("vector", ["ps_ms"], ["ms_acc"],
       lambda e: (e.tensor_copy(out=ms_acc[:, 0:512], in_=ps_ms[0:1, :]),
                  e.tensor_copy(out=ms_acc[:, 512:1024], in_=ps_ms[64:65, :]))[-1])

    st("scalar", ["ms_acc", "epsrms"], ["rstd_r"],
       lambda e: (e.activation(out=rstd_r[:], in_=ms_acc[:], func=AF.Ln,
                               bias=epsrms[:], scale=1.0 / 8192),
                  e.activation(out=rstd_r[:], in_=rstd_r[:], func=AF.Exp,
                               scale=-0.5))[-1])

    def mm_rstd(e):
        e.matmul(out=ps_h[:, 0:512], lhsT=ones_row[:], rhs=rstd_r[:, 0:512],
                 start=True, stop=True)
        return e.matmul(out=ps_h[:, 512:1024], lhsT=ones_row[:],
                        rhs=rstd_r[:, 512:1024], start=True, stop=True)
    st("tensor", ["rstd_r", "ones_row"], ["ps_h0", "ps_h1"], mm_rstd)
    st("vector", ["ps_h0", "ps_h1"], ["rstd128"],
       lambda e: e.tensor_copy(out=rstd128[:], in_=ps_h[:, 0:NTOK]))
    if DEBUG:
        st("gpsimd", ["rstd128"], [],
           lambda e: e.dma_start(out=dbg["rstd"], in_=rstd128[:]), dma="dbg")

    ph4.close()
    sch.fence()

    if PHASES < 5:
        st("gpsimd", [f"acc{m}" for m in range(16)] + ["rstd128"], ["odram"],
           lambda e: e.dma_start(
               out=o_fm.rearrange("(c p) q -> c p q", p=P).rearrange("c p q -> p c q"),
               in_=acc[:].rearrange("p (c q) -> p c q", c=NCH)),
           dma="ost")
        fv = sch.dma["ost"][1]
        sch.prog["gpsimd"].append(
            lambda e, fv=fv: e.wait_ge(sch.dma["ost"][0], fv))
        sch.emit()
        root.close()
        return nc, sch

    ph5 = ExitStack()
    Sp = ph5.enter_context
    w2b = [Sp(nc.sbuf_tensor(f"w2b{i}", [P, 4 * 512], BF16)) for i in range(2)]
    u_ld = [Sp(nc.sbuf_tensor(f"u_ld{i}", [P, 4 * 1024], BF16)) for i in range(2)]
    tmul = Sp(nc.sbuf_tensor("tmul", [P, 1024], F32))
    ps_p = Sp(nc.psum_tensor("ps_p", [P, 4 * 1024], F32))
    w2q_r = w2q_scr.rearrange("(c p) q -> c p q", p=P)
    u_r = u_scr.rearrange("(c p) q -> c p q", p=P)

    uall = [f"udram{s}_{m}" for s in range(W1_SLABS) for m in range(4)]
    w2all = [f"w2qdram{s}_{c}" for s in range(W2_SLABS) for c in range(64)]

    for dq in range(4):
        for kg in range(16):
            b = kg % 2
            first = (dq == 0 and kg < 2)
            st("sync", w2all if first else [], [f"w2b{b}"],
               (lambda e, dq=dq, kg=kg, b=b: e.dma_start(
                   out=w2b[b][:].rearrange("p (c q) -> p c q", c=4),
                   in_=w2q_r[4 * kg:4 * kg + 4, :, 512 * dq:512 * (dq + 1)]
                       .rearrange("c p q -> p c q"))),
               dma=f"w2b{b}")
            st("sync", uall if first else [], [f"u_ld{b}"],
               (lambda e, kg=kg, b=b: e.dma_start(
                   out=u_ld[b][:].rearrange("p (c q) -> p c q", c=4),
                   in_=u_r[4 * kg:4 * kg + 4, :, :]
                       .rearrange("c p q -> p c q"))),
               dma=f"uld{b}")

            def mm_p(e, kg=kg, b=b):
                for cc in range(4):
                    for dm in range(4):
                        for th in range(2):
                            i = e.matmul(
                                out=ps_p[:, dm * 1024 + th * 512:
                                         dm * 1024 + (th + 1) * 512],
                                lhsT=w2b[b][:, cc * 512 + dm * P:cc * 512 + (dm + 1) * P],
                                rhs=u_ld[b][:, cc * 1024 + th * 512:
                                            cc * 1024 + (th + 1) * 512],
                                start=(kg == 0 and cc == 0),
                                stop=(kg == 15 and cc == 3))
                return i
            st("tensor", [f"w2b{b}", f"u_ld{b}"],
               [f"ps_p{d}" for d in range(4)], mm_p)
        for dm in range(4):
            adm = 4 * dq + dm
            st("vector", [f"ps_p{dm}", "rstd128", f"acc{adm}"],
               [f"acc{adm}", "tmul"],
               (lambda e, dm=dm, adm=adm: (
                   e.tensor_mul(out=tmul[:], in0=ps_p[:, dm * 1024:(dm + 1) * 1024],
                                in1=rstd128[:]),
                   e.tensor_add(out=acc[:, adm * NTOK:(adm + 1) * NTOK],
                                in0=acc[:, adm * NTOK:(adm + 1) * NTOK],
                                in1=tmul[:]))[-1]))
    ph5.close()

    st("gpsimd", [f"acc{m}" for m in range(16)], ["odram"],
       lambda e: e.dma_start(
           out=o_fm.rearrange("(c p) q -> c p q", p=P).rearrange("c p q -> p c q"),
           in_=acc[:].rearrange("p (c q) -> p c q", c=NCH)),
       dma="ost")
    fv = sch.dma["ost"][1]
    sch.prog["gpsimd"].append(
        lambda e, fv=fv: e.wait_ge(sch.dma["ost"][0], fv))

    sch.emit()
    root.close()
    return nc, sch


def kernel(L0_attn_x, L0_attn_qkv_w, L0_attn_qkv_b, L0_attn_out_w,
           L0_attn_out_b, L0_attn_sinks, L0_ffn_W_fc, L0_ffn_W_proj):
    sh, cores = _host_prep(L0_attn_x, L0_attn_qkv_w, L0_attn_qkv_b,
                           L0_attn_out_w, L0_attn_out_b, L0_attn_sinks,
                           L0_ffn_W_fc, L0_ffn_W_proj)
    nc, _ = build_nc()
    in_maps = []
    for cid in range(8):
        m = dict(sh)
        m.update(cores[cid])
        in_maps.append(m)
    res = run_bass_kernel_spmd(nc, in_maps, core_ids=list(range(8)))
    out = np.empty((2, 4096, DMODEL), np.float32)
    for cid in range(8):
        b, j = cid // 4, cid % 4
        out[b, j * NTOK:(j + 1) * NTOK] = np.asarray(res.results[cid]["o_fm"]).T
    return out


# version 21
# speedup vs baseline: 1.0792x; 1.0104x over previous
import math
import numpy as np
import ml_dtypes
from contextlib import ExitStack

import concourse.bass as bass
import concourse.mybir as mybir
from concourse.bass_utils import run_bass_kernel_spmd

F32 = mybir.dt.float32
BF16 = mybir.dt.bfloat16
AF = mybir.ActivationFunctionType
ALU = mybir.AluOpType
AX = mybir.AxisListType
bf = ml_dtypes.bfloat16

P = 128
DMODEL = 2048
NCH = 16
TOKW = 1152
NTOK = 1024
HALO = 128
NKV = 8
LN_EPS = 1e-5
RMS_EPS = 1.1920929e-07
W1_SLABS = 16
W2_SLABS = 4
DEBUG = False
PHASES = 99
PH3_LEVEL = 9


def _rope_cos_sin(t_len):
    HEAD_DIM = 64
    ROPE_THETA = 150000.0
    SCALING = 32.0
    NTK_ALPHA = 1.0
    NTK_BETA = 32.0
    ICL = 1024
    d_half = HEAD_DIM // 2
    freq = ROPE_THETA ** (np.arange(0, HEAD_DIM, 2, dtype=np.float32) / HEAD_DIM)
    concentration = 0.1 * math.log(SCALING) + 1.0
    low = d_half * math.log(ICL / (NTK_BETA * 2 * math.pi)) / math.log(ROPE_THETA)
    high = d_half * math.log(ICL / (NTK_ALPHA * 2 * math.pi)) / math.log(ROPE_THETA)
    interpolation = 1.0 / (SCALING * freq)
    extrapolation = 1.0 / freq
    ramp = np.clip((np.arange(d_half, dtype=np.float32) - low) / (high - low), 0.0, 1.0)
    mask = 1.0 - ramp
    inv_freq = interpolation * (1.0 - mask) + extrapolation * mask
    freqs = np.outer(np.arange(t_len, dtype=np.float32), inv_freq)
    return np.cos(freqs) * concentration, np.sin(freqs) * concentration


def _host_prep(x, qkv_w, qkv_b, out_w, out_b, sinks, W_fc, W_proj):
    f32 = np.float32
    sh = {}
    sh["qkv_wT"] = np.ascontiguousarray(qkv_w.T.astype(f32, copy=False))
    sh["qkv_bT"] = np.ascontiguousarray(
        qkv_b.astype(f32, copy=False)[:2560].reshape(20, P).T)
    sh["vbias"] = np.ascontiguousarray(
        qkv_b.astype(f32, copy=False)[2560:].reshape(1, 512))
    sh["out_wT"] = np.ascontiguousarray(out_w.T.astype(f32, copy=False))
    sh["out_bC"] = np.ascontiguousarray(
        out_b.astype(f32, copy=False).reshape(16, P).T)
    sh["sinks128"] = np.ascontiguousarray(
        np.broadcast_to(sinks.astype(f32, copy=False), (P, 32)))
    sh["w1T"] = np.ascontiguousarray(W_fc.T.astype(f32, copy=False))
    sh["w2T"] = np.ascontiguousarray(W_proj.T.astype(f32, copy=False))
    sh["ident"] = np.eye(P, dtype=bf)
    sh["ones_col"] = np.ones((P, 1), f32)
    sh["ones_row"] = np.ones((1, P), f32)
    sperm = np.zeros((P, P), f32)
    for m in range(P):
        if m % 64 < 32:
            sperm[m + 32, m] = -1.0
        else:
            sperm[m - 32, m] = 1.0
    sh["sperm"] = sperm
    r = np.arange(P)[:, None]
    c = np.arange(P)[None, :]
    maskP = (c > r).astype(bf)
    maskD = (c <= r).astype(bf)
    sh["maskPD"] = np.ascontiguousarray(np.concatenate([maskP, maskD], axis=1))
    maskPD0 = sh["maskPD"].copy()
    maskPD0[:, :P] = 0
    gp = np.zeros((P, 16 * 32), dtype=bf)
    for cc in range(16):
        for p in range(P):
            gp[p, 32 * cc + 2 * cc + p // 64] = 1.0
    sh["gp"] = gp
    ep = np.zeros((32, 16 * P), f32)
    for cc in range(16):
        for p in range(P):
            ep[2 * cc + p // 64, P * cc + p] = 1.0
    sh["ep"] = ep
    sh["epb"] = ep.astype(bf)
    sh["epb2"] = np.concatenate([ep, ep], axis=0).astype(bf)

    cos_t, sin_t = _rope_cos_sin(4096)
    cores = []
    for cid in range(8):
        b, j = cid // 4, cid % 4
        t0 = NTOK * j
        d = {}
        xb = x[b].astype(f32, copy=False)
        xc = np.zeros((TOKW, DMODEL), f32)
        lo = max(0, t0 - HALO)
        xc[HALO - (t0 - lo):] = xb[lo:t0 + NTOK]
        d["x_fm"] = np.ascontiguousarray(xc.T)
        pos = np.arange(t0 - HALO, t0 + NTOK)
        posc = np.clip(pos, 0, 4095)
        ct = cos_t[posc].copy()
        st_ = sin_t[posc].copy()
        ct[pos < 0] = 0.0
        st_[pos < 0] = 0.0
        idx = np.arange(P) % 32
        d["cos128"] = np.ascontiguousarray(ct[:, idx].T)
        d["sin128"] = np.ascontiguousarray(st_[:, idx].T)
        d["qcos128"] = np.ascontiguousarray(ct[128:, idx].T * np.float32(0.125))
        d["qsin128"] = np.ascontiguousarray(st_[128:, idx].T * np.float32(0.125))
        d["maskPD0"] = maskPD0 if j == 0 else sh["maskPD"]
        cores.append(d)
    return sh, cores


class _Res:
    __slots__ = ("name", "w", "readers", "strict")

    def __init__(self, name):
        self.name = name
        self.w = None
        self.readers = {}
        self.strict = False


class Sched:
    ENGS = ("sync", "vector", "tensor", "scalar", "gpsimd")

    def __init__(self, nc, S):
        self.nc = nc
        self.prog = {e: [] for e in self.ENGS}
        self.trk = {e: [S(nc.semaphore("m_" + e)), 0] for e in self.ENGS}
        self.dma = {}
        self.S = S
        self.res = {}
        self.wm = {}
        self.floor = {}
        self.nsteps = 0
        self.nwaits = 0
        self.nfence = 0

    def r(self, name):
        if name not in self.res:
            self.res[name] = _Res(name)
        return self.res[name]

    def _sem(self, semname):
        if semname.startswith("d:"):
            return self.dma[semname[2:]][0]
        return self.trk[semname][0]

    def step(self, eng, reads, writes, fn, dma=None):
        reads = [self.r(n) for n in reads]
        writes = [self.r(n) for n in writes]
        waits = {}

        def need(dep, strict=False):
            if dep is None:
                return
            deng, dsem, dval = dep
            if deng == eng and not dsem.startswith("d:") and not strict:
                return
            key = (eng, dsem)
            if self.wm.get(key, -1) >= dval:
                return
            if waits.get(dsem, -1) < dval:
                waits[dsem] = dval

        for fsem, fval in self.floor.items():
            need(("_fence", fsem, fval))
        for rr in reads:
            need(rr.w, rr.strict)
        for ww in writes:
            need(ww.w, ww.strict)
            for dep in ww.readers.values():
                need(dep, ww.strict)
        if dma is not None:
            if dma not in self.dma:
                self.dma[dma] = [self.S(self.nc.semaphore("d_" + dma)), 0]
            t = self.dma[dma]
            t[1] += 16
            semname, val, inc = "d:" + dma, t[1], 16
        else:
            t = self.trk[eng]
            t[1] += 1
            semname, val, inc = eng, t[1], 1
        for ww in writes:
            ww.w = (eng, semname, val)
            ww.readers = {}
        for rr in reads:
            rr.readers[eng] = (eng, semname, val)
        for dsem, dval in waits.items():
            self.wm[(eng, dsem)] = dval
        wl = sorted(waits.items())
        self.nsteps += 1
        self.nwaits += len(wl)
        sem_h = self._sem(semname)

        def go(engobj, wl=wl, fn=fn, sem_h=sem_h, inc=inc):
            for dsem, dval in wl:
                engobj.wait_ge(self._sem(dsem), dval)
            fn(engobj).then_inc(sem_h, inc)

        self.prog[eng].append(go)

    def strict(self, *names):
        for n in names:
            self.r(n).strict = True

    def fence(self):
        self.nfence += 1
        allres = list(self.res.keys())
        marks = {}
        for e in self.ENGS:
            self.step(e, allres, [f"_f{self.nfence}_{e}"],
                      lambda x: x.nop())
            marks[e] = self.trk[e][1]
        self.floor = dict(marks)

    def emit(self):
        with self.nc.Block() as blk:
            @blk.sync
            def _(e):
                for g in self.prog["sync"]:
                    g(e)

            @blk.vector
            def _(e):
                for g in self.prog["vector"]:
                    g(e)

            @blk.tensor
            def _(e):
                for g in self.prog["tensor"]:
                    g(e)

            @blk.scalar
            def _(e):
                for g in self.prog["scalar"]:
                    g(e)

            @blk.gpsimd
            def _(e):
                for g in self.prog["gpsimd"]:
                    g(e)


def build_nc():
    nc = bass.Bass("TRN2", target_bir_lowering=False, debug=False,
                   detect_race_conditions=False)

    def inp(name, shape, dt=F32):
        return nc.dram_tensor(name, shape, dt, kind="ExternalInput").ap()

    x_fm = inp("x_fm", [DMODEL, TOKW])
    qkv_wT_d = inp("qkv_wT", [DMODEL, 3072])
    qkv_bT_d = inp("qkv_bT", [P, 20])
    vbias_d = inp("vbias", [1, 512])
    out_wT_d = inp("out_wT", [DMODEL, DMODEL])
    out_bC_d = inp("out_bC", [P, 16])
    sinks_d = inp("sinks128", [P, 32])
    w1T_d = inp("w1T", [DMODEL, 8192])
    w2T_d = inp("w2T", [8192, DMODEL])
    cos_d = inp("cos128", [P, TOKW])
    sin_d = inp("sin128", [P, TOKW])
    qcos_d = inp("qcos128", [P, NTOK])
    qsin_d = inp("qsin128", [P, NTOK])
    ident_d = inp("ident", [P, P], BF16)
    onesc_d = inp("ones_col", [P, 1])
    onesr_d = inp("ones_row", [1, P])
    sperm_d = inp("sperm", [P, P])
    maskPD_d = inp("maskPD", [P, 256], BF16)
    maskPD0_d = inp("maskPD0", [P, 256], BF16)
    gp_d = inp("gp", [P, 512], BF16)
    ep_d = inp("ep", [32, 2048])
    epb_d = inp("epb", [32, 2048], BF16)
    epb2_d = inp("epb2", [64, 2048], BF16)

    o_fm = nc.dram_tensor("o_fm", [DMODEL, NTOK], F32, kind="ExternalOutput").ap()
    dbg = {}
    if DEBUG:
        for nm, shp, dt in [("xn", [P, NCH * TOKW], BF16),
                            ("kb", [P, NKV * TOKW], BF16),
                            ("vt", [P, 9 * 512], BF16),
                            ("attn0", [P, NCH * 512], BF16),
                            ("attn1", [P, NCH * 512], BF16),
                            ("acc1", [P, NCH * NTOK], F32),
                            ("acc2", [P, NCH * NTOK], F32),
                            ("rstd", [P, NTOK], F32),
                            ("sc0", [P, 512], F32), ("sc1", [P, 512], F32),
                            ("smal", [P, 20], F32),
                            ("praw", [P, 1024], BF16),
                            ("ptsb", [P, 1024], BF16)]:
            dbg[nm] = nc.dram_tensor("dbg_" + nm, shp, dt,
                                     kind="ExternalOutput").ap()
    u_scr = nc.dram_tensor("u_scr", [8192, NTOK], BF16, kind="Internal").ap()
    w2q_scr = nc.dram_tensor("w2q_scr", [8192, DMODEL], BF16, kind="Internal").ap()

    root = ExitStack()
    S = root.enter_context
    sch = Sched(nc, S)
    st = sch.step
    sch.strict("ms_acc", "scal_t", "rs_t",
               "rs_hl", "rs_lo", "mean", "work", "mrs_r", "rstd_r")
    for _pp in range(2):
        sch.strict(f"rowsum{_pp}", f"rden{_pp}", f"den_t{_pp}",
                   f"m_neg{_pp}", f"es_t{_pp}")

    xn = S(nc.sbuf_tensor("xn", [P, NCH * TOKW], BF16))
    acc = S(nc.sbuf_tensor("acc", [P, NCH * NTOK], F32))
    identt = S(nc.sbuf_tensor("identt", [P, P], BF16))
    ones_col = S(nc.sbuf_tensor("ones_colt", [P, 1], F32))
    ones_row = S(nc.sbuf_tensor("ones_rowt", [1, P], F32))
    spermt = S(nc.sbuf_tensor("spermt", [P, P], F32))
    maskPD = S(nc.sbuf_tensor("maskPDt", [P, 256], BF16))
    maskPD0 = S(nc.sbuf_tensor("maskPD0t", [P, 256], BF16))
    qkvbT = S(nc.sbuf_tensor("qkvbT", [P, 20], F32))
    outbC = S(nc.sbuf_tensor("outbC", [P, 16], F32))
    vbias_t = S(nc.sbuf_tensor("vbias_t", [1, 512], F32))
    sinks_t = S(nc.sbuf_tensor("sinks_t", [P, 32], F32))
    epsln = S(nc.sbuf_tensor("epsln", [1, 1], F32))
    epsrms = S(nc.sbuf_tensor("epsrms", [1, 1], F32))
    rstd128 = S(nc.sbuf_tensor("rstd128", [P, NTOK], F32))

    NSL = ((0, 512), (512, 1024), (1024, TOKW))
    xnr = [f"xn{c}" for c in range(NCH)]

    for dst, src, nm in [(identt, ident_d, "identt"), (ones_col, onesc_d, "ones_col"),
                         (ones_row, onesr_d, "ones_row"), (spermt, sperm_d, "spermt"),
                         (maskPD, maskPD_d, "maskPD"), (maskPD0, maskPD0_d, "maskPD0"),
                         (qkvbT, qkv_bT_d, "qkvbT"), (outbC, out_bC_d, "outbC"),
                         (vbias_t, vbias_d, "vbias"), (sinks_t, sinks_d, "sinks")]:
        st("sync", [], [nm],
           (lambda e, dst=dst, src=src: e.dma_start(out=dst[:], in_=src)),
           dma="tab_" + nm)
    onesb = S(nc.sbuf_tensor("onesb", [P, 1], BF16))
    st("vector", [], ["onesb"], lambda e: e.memset(onesb[:], 1.0))
    st("vector", [], ["epsln", "epsrms"],
       lambda e: (e.memset(epsln[:], LN_EPS), e.memset(epsrms[:], RMS_EPS))[-1])

    ph1 = ExitStack()
    Sp = ph1.enter_context
    xst = [Sp(nc.sbuf_tensor(f"xst{i}", [P, TOKW], F32)) for i in range(3)]
    xsq = Sp(nc.sbuf_tensor("xsq", [P, TOKW], F32))
    statb = Sp(nc.sbuf_tensor("statb", [1, 3 * TOKW], F32))
    mean = statb[:, 0:TOKW]
    work = statb[:, TOKW:2 * TOKW]
    mrs_r = statb[:, 2 * TOKW:3 * TOKW]
    m2 = var = lnv = rs_r = work

    with ExitStack() as ph1a:
        ps_sum = ph1a.enter_context(nc.psum_tensor("ps_sum", [1, TOKW], F32))
        ps_sq = ph1a.enter_context(nc.psum_tensor("ps_sq", [1, TOKW], F32))
        for c in range(NCH):
            bn = f"xst{c % 3}"
            st("sync", [], [bn],
               (lambda e, c=c: e.dma_start(out=xst[c % 3][:],
                                           in_=x_fm[c * P:(c + 1) * P, :])),
               dma=bn)
            st("scalar", [bn], ["xsq"],
               (lambda e, c=c: e.activation(out=xsq[:], in_=xst[c % 3][:],
                                            func=AF.Square)))

            def mm_sums(e, c=c):
                for (lo, hi) in NSL:
                    e.matmul(out=ps_sum[:, lo:hi], lhsT=ones_col[:],
                             rhs=xst[c % 3][:, lo:hi], start=(c == 0),
                             stop=(c == NCH - 1))
                    i = e.matmul(out=ps_sq[:, lo:hi], lhsT=ones_col[:],
                                 rhs=xsq[:, lo:hi], start=(c == 0),
                                 stop=(c == NCH - 1))
                return i
            st("tensor", [bn, "xsq", "ones_col"], ["ps_sum", "ps_sq"], mm_sums)
            st("vector", [bn], [f"acc{c}"],
               (lambda e, c=c: e.tensor_copy(out=acc[:, c * NTOK:(c + 1) * NTOK],
                                             in_=xst[c % 3][:, HALO:])))
        st("vector", ["ps_sum", "ps_sq"], ["mean", "work", "mrs_r"],
           lambda e: (e.tensor_scalar(out=mean, in0=ps_sum[:], scalar1=1.0 / DMODEL,
                                      scalar2=None, op0=ALU.mult),
                      e.tensor_scalar(out=work, in0=ps_sq[:], scalar1=1.0 / DMODEL,
                                      scalar2=None, op0=ALU.mult),
                      e.scalar_tensor_tensor(out=mrs_r, in0=mean, scalar=-1.0,
                                             in1=mean, op0=ALU.mult, op1=ALU.mult),
                      e.tensor_add(out=work, in0=work, in1=mrs_r))[-1])
        st("scalar", ["work", "epsln"], ["work"],
           lambda e: (e.activation(out=lnv, in_=var, func=AF.Ln, bias=epsln[:]),
                      e.activation(out=rs_r, in_=lnv, func=AF.Exp, scale=-0.5))[-1])
        st("vector", ["work", "mean"], ["mrs_r"],
           lambda e: e.tensor_mul(out=mrs_r, in0=mean, in1=rs_r))

    sch.fence()
    ps_rs1 = Sp(nc.psum_tensor("ps_rs1", [P, TOKW], F32))
    ps_mrs = Sp(nc.psum_tensor("ps_mrs", [P, TOKW], F32))

    def mm_bcast(e):
        for (lo, hi) in NSL:
            e.matmul(out=ps_rs1[:, lo:hi], lhsT=ones_row[:], rhs=rs_r[:, lo:hi],
                     start=True, stop=True)
            i = e.matmul(out=ps_mrs[:, lo:hi], lhsT=ones_row[:],
                         rhs=mrs_r[:, lo:hi], start=True, stop=True)
        return i
    st("tensor", ["work", "mrs_r", "ones_row"], ["ps_rs1", "ps_mrs"], mm_bcast)

    for c in range(NCH):
        bn = f"xst{c % 3}"
        st("sync", [], [bn],
           (lambda e, c=c: e.dma_start(out=xst[c % 3][:, 0:P],
                                       in_=x_fm[c * P:(c + 1) * P, 0:P])),
           dma=bn)
        st("vector", [bn, "ps_rs1", "ps_mrs", f"acc{c}"], [f"xn{c}", "xsq"],
           (lambda e, c=c: (
               e.tensor_mul(out=xsq[:, 0:P], in0=xst[c % 3][:, 0:P],
                            in1=ps_rs1[:, 0:P]),
               e.tensor_sub(out=xn[:, c * TOKW:c * TOKW + P],
                            in0=xsq[:, 0:P], in1=ps_mrs[:, 0:P]),
               e.tensor_mul(out=xsq[:, P:TOKW],
                            in0=acc[:, c * NTOK:(c + 1) * NTOK],
                            in1=ps_rs1[:, P:TOKW]),
               e.tensor_sub(out=xn[:, c * TOKW + P:(c + 1) * TOKW],
                            in0=xsq[:, P:TOKW],
                            in1=ps_mrs[:, P:TOKW]))[-1]))
    if DEBUG:
        st("gpsimd", [f"acc{m}" for m in range(16)], [],
           lambda e: e.dma_start(out=dbg["acc1"], in_=acc[:]), dma="dbg")
        st("gpsimd", xnr, [],
           lambda e: e.dma_start(out=dbg["xn"], in_=xn[:]), dma="dbg")
    ph1.close()
    sch.fence()

    qkv_r = qkv_wT_d.rearrange("(c p) q -> c p q", p=P)
    out_r = out_wT_d.rearrange("(c p) q -> c p q", p=P)
    w1_r = w1T_d.rearrange("(c p) q -> c p q", p=P)
    w2_r = w2T_d.rearrange("(c p) q -> c p q", p=P)

    def load_wslab(src_r, colbase, width, half, wt):
        for cp in range(4):
            st("gpsimd", [], [f"wbf{half}"],
               (lambda e, cp=cp, colbase=colbase, width=width, src_r=src_r,
                        w=wt[half]:
                e.dma_start(
                    out=w[:, 4 * cp * width:(4 * cp + 4) * width].rearrange(
                        "p (c q) -> p c q", c=4),
                    in_=src_r[4 * cp:4 * cp + 4, :, colbase:colbase + width]
                        .rearrange("c p q -> p c q"))),
               dma=f"wld{half}")

    if PHASES < 2:
        st("gpsimd", [f"acc{m}" for m in range(16)] + xnr, ["odram"],
           lambda e: e.dma_start(
               out=o_fm.rearrange("(c p) q -> c p q", p=P).rearrange("c p q -> p c q"),
               in_=acc[:].rearrange("p (c q) -> p c q", c=NCH)),
           dma="ost")
        fv = sch.dma["ost"][1]
        sch.prog["gpsimd"].append(
            lambda e, fv=fv: e.wait_ge(sch.dma["ost"][0], fv))
        sch.emit()
        root.close()
        return nc, sch

    ph23 = ExitStack()
    S23 = ph23.enter_context
    k_both = S23(nc.sbuf_tensor("k_both", [P, NKV * TOKW], BF16))
    v_t = S23(nc.sbuf_tensor("v_t", [P, 9 * 512], BF16))
    qcost = S23(nc.sbuf_tensor("qcost", [P, NTOK], F32))
    qsint = S23(nc.sbuf_tensor("qsint", [P, NTOK], F32))
    st("sync", [], ["qcost"],
       lambda e: e.dma_start(out=qcost[:], in_=qcos_d), dma="tab_qcost")
    st("sync", [], ["qsint"],
       lambda e: e.dma_start(out=qsint[:], in_=qsin_d), dma="tab_qsint")
    ph2 = ExitStack()
    Sp = ph2.enter_context
    cost = Sp(nc.sbuf_tensor("cost", [P, TOKW], F32))
    sint = Sp(nc.sbuf_tensor("sint", [P, TOKW], F32))
    st("sync", [], ["cost"],
       lambda e: e.dma_start(out=cost[:], in_=cos_d), dma="tab_cost")
    st("sync", [], ["sint"],
       lambda e: e.dma_start(out=sint[:], in_=sin_d), dma="tab_sint")
    kf32 = Sp(nc.sbuf_tensor("kf32", [P, TOKW], F32))
    t1s = Sp(nc.sbuf_tensor("t1s", [P, TOKW], F32))
    krtmp = Sp(nc.sbuf_tensor("krtmp", [P, TOKW], BF16))
    ps_k = Sp(nc.psum_tensor("ps_k", [P, TOKW], F32))
    ps_rot = Sp(nc.psum_tensor("ps_rot", [P, TOKW], F32))
    ps_v = Sp(nc.psum_tensor("ps_v", [P, 512], F32))
    wbf2 = [Sp(nc.sbuf_tensor(f"wbf2_{i}", [P, 8192], BF16)) for i in range(2)]

    load_wslab(qkv_r, 2048, 512, 0, wbf2)
    load_wslab(qkv_r, 2560, 512, 1, wbf2)

    for m in range(4):
        def mm_k(e, m=m, w=None):
            w = wbf2[0]
            for c in range(NCH):
                for (lo, hi) in NSL:
                    i = e.matmul(
                        out=ps_k[:, lo:hi],
                        lhsT=w[:, c * 512 + m * P:c * 512 + (m + 1) * P],
                        rhs=xn[:, c * TOKW + lo:c * TOKW + hi],
                        start=(c == 0), stop=(c == NCH - 1))
            return i
        st("tensor", ["wbf0"] + xnr, ["ps_k"], mm_k)
        st("scalar", ["ps_k", "qkvbT"], ["kf32"],
           (lambda e, m=m: e.activation(out=kf32[:], in_=ps_k[:], func=AF.Identity,
                                        bias=qkvbT[:, 16 + m:17 + m])))

        def mm_rot(e):
            for (lo, hi) in NSL:
                i = e.matmul(out=ps_rot[:, lo:hi], lhsT=spermt[:],
                             rhs=kf32[:, lo:hi], start=True, stop=True)
            return i
        st("tensor", ["kf32", "spermt"], ["ps_rot"], mm_rot)
        st("vector", ["kf32", "ps_rot", "cost", "sint"], ["krtmp", "t1s", "kf32"],
           (lambda e, m=m: (
               e.tensor_mul(out=t1s[:], in0=kf32[:], in1=cost[:]),
               e.tensor_mul(out=kf32[:], in0=ps_rot[:], in1=sint[:]),
               e.tensor_add(out=krtmp[:], in0=t1s[:], in1=kf32[:]))[-1]))
        for gg in range(2):
            g = 2 * m + gg
            bsrc = 64 * gg
            st("vector", ["krtmp"], [f"kb{g}"],
               (lambda e, g=g, bsrc=bsrc: (
                   e.tensor_copy(out=k_both[0:64, g * TOKW:(g + 1) * TOKW],
                                 in_=krtmp[bsrc:bsrc + 64, :]),
                   e.tensor_copy(out=k_both[64:128, g * TOKW:(g + 1) * TOKW],
                                 in_=krtmp[bsrc:bsrc + 64, :]))[-1]))
    for tt in range(9):
        def mm_v(e, tt=tt):
            for c in range(NCH):
                e.matmul(out=ps_v[:],
                         lhsT=xn[:, c * TOKW + tt * P:c * TOKW + (tt + 1) * P],
                         rhs=wbf2[1][:, c * 512:(c + 1) * 512],
                         start=(c == 0), stop=False)
            return e.matmul(out=ps_v[:], lhsT=ones_row[:], rhs=vbias_t[:],
                            start=False, stop=True)
        st("tensor", ["wbf1", "vbias", "ones_row"] + xnr, ["ps_v"], mm_v)
        st("scalar", ["ps_v"], [f"vt{tt}"],
           (lambda e, tt=tt: e.copy(out=v_t[:, tt * 512:(tt + 1) * 512],
                                    in_=ps_v[:])))
    if DEBUG:
        st("gpsimd", [f"kb{g}" for g in range(NKV)], [],
           lambda e: e.dma_start(out=dbg["kb"], in_=k_both[:]), dma="dbg")
        st("gpsimd", [f"vt{t}" for t in range(9)], [],
           lambda e: e.dma_start(out=dbg["vt"], in_=v_t[:]), dma="dbg")
    ph2.close()
    sch.fence()

    if PHASES < 3:
        ph23.close()
        st("gpsimd", [f"acc{m}" for m in range(16)] + [f"kb{g}" for g in range(8)] + [f"vt{t}" for t in range(9)], ["odram"],
           lambda e: e.dma_start(
               out=o_fm.rearrange("(c p) q -> c p q", p=P).rearrange("c p q -> p c q"),
               in_=acc[:].rearrange("p (c q) -> p c q", c=NCH)),
           dma="ost")
        fv = sch.dma["ost"][1]
        sch.prog["gpsimd"].append(
            lambda e, fv=fv: e.wait_ge(sch.dma["ost"][0], fv))
        sch.emit()
        root.close()
        return nc, sch

    ph3 = ExitStack()
    Sp = ph3.enter_context
    qf32 = Sp(nc.sbuf_tensor("qf32", [P, 512], F32))
    t1q = Sp(nc.sbuf_tensor("t1q", [P, 512], F32))
    qr = Sp(nc.sbuf_tensor("qr", [P, 1024], BF16))
    qz = Sp(nc.sbuf_tensor("qz", [P, 4096], BF16))
    attn_h = Sp(nc.sbuf_tensor("attn_h", [P, NCH * 512], BF16))
    praw = Sp(nc.sbuf_tensor("praw", [P, 2048], BF16))
    pt_sb = Sp(nc.sbuf_tensor("pt_sb", [P, 1024], BF16))
    smal = Sp(nc.sbuf_tensor("smal", [P, 40], F32))
    ps_q = Sp(nc.psum_tensor("ps_q", [P, 512], F32))
    ps_rq = Sp(nc.psum_tensor("ps_rq", [P, 512], F32))
    ps_s = Sp(nc.psum_tensor("ps_s", [P, 2048], F32))
    ps_t = Sp(nc.psum_tensor("ps_t", [P, 1024], BF16))
    ps_av = Sp(nc.psum_tensor("ps_av", [P, 256], F32))
    ps_o = ps_rq
    wbf3 = [Sp(nc.sbuf_tensor(f"wbf3_{i}", [P, 4096], BF16)) for i in range(2)]

    st("vector", [], ["praw0", "praw1"], lambda e: e.memset(praw[:], 0.0))
    st("vector", [], ["qz0", "qz1"], lambda e: e.memset(qz[:], 0.0))

    pend = []

    for half in range(2):
        hlo = HALO + 512 * half
        def qprep_pieces(g, hlo=hlo, half=half):
            gp_ = g % 2
            wq = g % 2
            pcs = []
            for mq in range(2):
                def p_mm(mq=mq, wq=wq, hlo=hlo, g=g):
                    def mm_q(e, mq=mq, hlo=hlo, wq=wq):
                        for c in range(NCH):
                            i = e.matmul(
                                out=ps_q[:],
                                lhsT=wbf3[wq][:, c * 256 + mq * P:
                                              c * 256 + (mq + 1) * P],
                                rhs=xn[:, c * TOKW + hlo:c * TOKW + hlo + 512],
                                start=(c == 0), stop=(c == NCH - 1))
                        return i
                    st("tensor", [f"wbf{wq}"] + xnr, ["ps_q"], mm_q)
                    st("scalar", ["ps_q", "qkvbT"], ["qf32"],
                       (lambda e, g=g, mq=mq: e.activation(
                           out=qf32[:], in_=ps_q[:],
                           func=AF.Identity,
                           bias=qkvbT[:, 2 * g + mq:2 * g + mq + 1])))

                def p_rope(mq=mq, half=half, gp_=gp_, last=(mq == 1)):
                    st("tensor", ["qf32", "spermt"], ["ps_rq"],
                       lambda e: e.matmul(out=ps_rq[:], lhsT=spermt[:],
                                          rhs=qf32[:], start=True, stop=True))
                    st("vector", ["qf32", "ps_rq", "qcost", "qsint"],
                       [f"qr{mq}", "t1q", "qf32"],
                       (lambda e, mq=mq, half=half: (
                           e.tensor_mul(out=t1q[:], in0=qf32[:],
                                        in1=qcost[:, half * 512:(half + 1) * 512]),
                           e.tensor_mul(out=qf32[:], in0=ps_rq[:],
                                        in1=qsint[:, half * 512:(half + 1) * 512]),
                           e.tensor_add(out=qr[:, mq * 512:(mq + 1) * 512],
                                        in0=t1q[:], in1=qf32[:]))[-1]))
                    if last:
                        def do_qz(e, gp_=gp_):
                            for h in range(4):
                                hb = 64 * (h % 2)
                                i = e.tensor_copy(
                                    out=qz[hb:hb + 64,
                                           2048 * gp_ + 512 * h:
                                           2048 * gp_ + 512 * (h + 1)],
                                    in_=qr[hb:hb + 64,
                                           (h // 2) * 512:(h // 2 + 1) * 512])
                            return i
                        st("vector", ["qr0", "qr1"], [f"qz{gp_}"], do_qz)
                pcs.append(p_mm)
                pcs.append(p_rope)
            return pcs

        qpp = []
        for g in range(NKV):
            gp_ = g % 2
            if g == 0:
                load_wslab(qkv_r, 0, 256, 0, wbf3)
                for t_ in qprep_pieces(0):
                    t_()
            if g + 1 < NKV:
                load_wslab(qkv_r, 256 * (g + 1), 256, (g + 1) % 2, wbf3)
                qpp = qprep_pieces(g + 1)
            else:
                qpp = []
            for qt in range(4):
                if PH3_LEVEL < 1:
                    continue
                aqt = 4 * half + qt
                par = (4 * g + qt) % 2
                so = 1024 * par
                sm0 = 20 * par
                m_neg = smal[:, sm0:sm0 + 4]
                rowsum = smal[:, sm0 + 4:sm0 + 8]
                es_t = smal[:, sm0 + 8:sm0 + 12]
                den_t = smal[:, sm0 + 12:sm0 + 16]
                rden = smal[:, sm0 + 16:sm0 + 20]

                def mm_s(e, g=g, qt=qt, aqt=aqt, gp_=gp_, so=so):
                    for h in range(4):
                        kt = k_both[:, g * TOKW:(g + 1) * TOKW]
                        i = e.matmul(
                            out=ps_s[:, so + h * 256:so + (h + 1) * 256],
                            lhsT=qz[:, 2048 * gp_ + 512 * h + qt * P:
                                    2048 * gp_ + 512 * h + (qt + 1) * P],
                            rhs=kt[:, P * aqt:P * aqt + 256],
                            start=True, stop=True)
                    return i
                st("tensor", [f"qz{gp_}", f"kb{g}"], [f"ps_s{par}"], mm_s)
                if PH3_LEVEL < 2:
                    continue
                if pend:
                    pend[0]()
                st("vector", [f"ps_s{par}"], [f"m_neg{par}"],
                   (lambda e, so=so, m_neg=m_neg: e.tensor_reduce(
                       out=m_neg,
                       in_=ps_s[:, so:so + 1024].rearrange(
                           "p (h k) -> p h k", k=256),
                       axis=AX.X, op=ALU.max, negate=True)))
                for th_ in pend[1:]:
                    th_()
                pend = []

                def do_exp(e, so=so, m_neg=m_neg):
                    for h in range(4):
                        i = e.activation(
                            out=praw[:, so + h * 256:so + (h + 1) * 256],
                            in_=ps_s[:, so + h * 256:so + (h + 1) * 256],
                            func=AF.Exp, bias=m_neg[:, h:h + 1])
                    return i
                st("scalar", [f"ps_s{par}", f"m_neg{par}"], [f"praw{par}"], do_exp)

                def do_es(e, g=g, m_neg=m_neg, es_t=es_t):
                    for h in range(4):
                        i = e.activation(out=es_t[:, h:h + 1],
                                         in_=m_neg[:, h:h + 1], func=AF.Exp,
                                         bias=sinks_t[:, 4 * g + h:4 * g + h + 1])
                    return i
                st("scalar", [f"m_neg{par}", "sinks"], [f"es_t{par}"], do_es)
                mkn = "maskPD0" if aqt == 0 else "maskPD"
                mk = maskPD0 if aqt == 0 else maskPD

                def do_mask(e, mk=mk, so=so, rowsum=rowsum):
                    for h in range(4):
                        i = e.scalar_tensor_tensor(
                            out=praw[:, so + h * 256:so + (h + 1) * 256],
                            in0=praw[:, so + h * 256:so + (h + 1) * 256],
                            scalar=1.0,
                            in1=mk[:], op0=ALU.mult, op1=ALU.mult,
                            accum_out=rowsum[:, h:h + 1])
                    return i
                st("vector", [f"praw{par}", mkn],
                   [f"praw{par}", f"rowsum{par}"], do_mask)
                st("vector", [f"rowsum{par}", f"es_t{par}"], [f"den_t{par}"],
                   (lambda e, den_t=den_t, rowsum=rowsum, es_t=es_t:
                    e.tensor_add(out=den_t, in0=rowsum, in1=es_t)))
                st("vector", [f"den_t{par}"], [f"rden{par}"],
                   (lambda e, rden=rden, den_t=den_t:
                    e.reciprocal(out=rden, in_=den_t)))

                def do_norm(e, so=so, rden=rden):
                    for h in range(4):
                        i = e.tensor_scalar(
                            out=praw[:, so + h * 256:so + (h + 1) * 256],
                            in0=praw[:, so + h * 256:so + (h + 1) * 256],
                            scalar1=rden[:, h:h + 1],
                            scalar2=None, op0=ALU.mult)
                    return i
                st("vector", [f"rden{par}", f"praw{par}"], [f"praw{par}"],
                   do_norm)
                if PH3_LEVEL < 3:
                    continue

                def do_pt(e, so=so):
                    for h in range(4):
                        for k2 in range(2):
                            i = e.transpose(
                                out=ps_t[:, k2 * 512 + h * P:k2 * 512 + (h + 1) * P],
                                in_=praw[:, so + h * 256 + k2 * P:
                                         so + h * 256 + (k2 + 1) * P],
                                identity=identt[:])
                    return i

                def do_av(e, g=g, aqt=aqt):
                    for h in (0, 2, 1, 3):
                        ob = 64 * (h % 2)
                        oc = P * (h // 2)
                        e.matmul(out=ps_av[ob:ob + 64, oc:oc + P],
                                 lhsT=v_t[:, aqt * 512 + 64 * g:aqt * 512 + 64 * g + 64],
                                 rhs=pt_sb[:, h * P:(h + 1) * P],
                                 start=True, stop=False)
                        i = e.matmul(
                            out=ps_av[ob:ob + 64, oc:oc + P],
                            lhsT=v_t[:, (aqt + 1) * 512 + 64 * g:
                                     (aqt + 1) * 512 + 64 * g + 64],
                            rhs=pt_sb[:, 512 + h * P:512 + (h + 1) * P],
                            start=False, stop=True)
                    return i

                def do_avev(e, g=g, qt=qt):
                    for hh in range(2):
                        i = e.copy(
                            out=attn_h[:, (2 * g + hh) * 512 + qt * P:
                                       (2 * g + hh) * 512 + (qt + 1) * P],
                            in_=ps_av[:, hh * P:(hh + 1) * P])
                    return i
                pend = [
                    (lambda par=par, do_pt=do_pt: st(
                        "tensor", [f"praw{par}", "identt"], ["ps_t"], do_pt)),
                    (lambda: st("vector", ["ps_t"], ["pt_sb"],
                                lambda e: e.tensor_copy(out=pt_sb[:],
                                                        in_=ps_t[:]))),
                    (lambda aqt=aqt, do_av=do_av: st(
                        "tensor", ["pt_sb", f"vt{aqt}", f"vt{aqt + 1}"],
                        ["ps_av"], do_av)),
                    (lambda g=g, do_avev=do_avev: st(
                        "scalar", ["ps_av"],
                        [f"attnh{2 * g}", f"attnh{2 * g + 1}"], do_avev)),
                ]
                if qpp:
                    qpp.pop(0)()
            while qpp:
                qpp.pop(0)()
        if PH3_LEVEL < 5:
            continue
        for th_ in pend:
            th_()
        pend = []
        if DEBUG:
            st("gpsimd", [f"attnh{i}" for i in range(16)], [],
               (lambda e, half=half: e.dma_start(out=dbg["attn" + str(half)],
                                                 in_=attn_h[:])), dma="dbg")
        ares = [f"attnh{i}" for i in range(16)]
        for sl in range(8):
            wo = sl % 2
            load_wslab(out_r, 256 * sl, 256, wo, wbf3)
            for dm in (2 * sl, 2 * sl + 1):
                pst = ps_q if dm % 2 else ps_rq
                psn = "ps_q" if dm % 2 else "ps_rq"

                def mm_o(e, dm=dm, pst=pst, wo=wo):
                    for c in range(NCH):
                        i = e.matmul(
                            out=pst[:],
                            lhsT=wbf3[wo][:, c * 256 + (dm % 2) * P:
                                          c * 256 + (dm % 2 + 1) * P],
                            rhs=attn_h[:, c * 512:(c + 1) * 512],
                            start=(c == 0), stop=(c == NCH - 1))
                    return i
                st("tensor", [f"wbf{wo}"] + ares, [psn], mm_o)
                st("vector", [psn, f"acc{dm}"], [f"acc{dm}"],
                   (lambda e, dm=dm, half=half, pst=pst: e.tensor_add(
                       out=acc[:, dm * NTOK + 512 * half:
                               dm * NTOK + 512 * (half + 1)],
                       in0=acc[:, dm * NTOK + 512 * half:
                               dm * NTOK + 512 * (half + 1)],
                       in1=pst[:])))
    for dm in range(16):
        st("vector", [f"acc{dm}", "outbC"], [f"acc{dm}"],
           (lambda e, dm=dm: e.tensor_scalar(
               out=acc[:, dm * NTOK:(dm + 1) * NTOK],
               in0=acc[:, dm * NTOK:(dm + 1) * NTOK],
               scalar1=outbC[:, dm:dm + 1], scalar2=None, op0=ALU.add)))
    if DEBUG:
        st("gpsimd", [f"acc{m}" for m in range(16)], [],
           lambda e: e.dma_start(out=dbg["acc2"], in_=acc[:]), dma="dbg")
    ph3.close()
    ph23.close()
    sch.fence()

    if PHASES < 4:
        st("gpsimd", [f"acc{m}" for m in range(16)], ["odram"],
           lambda e: e.dma_start(
               out=o_fm.rearrange("(c p) q -> c p q", p=P).rearrange("c p q -> p c q"),
               in_=acc[:].rearrange("p (c q) -> p c q", c=NCH)),
           dma="ost")
        fv = sch.dma["ost"][1]
        sch.prog["gpsimd"].append(
            lambda e, fv=fv: e.wait_ge(sch.dma["ost"][0], fv))
        sch.emit()
        root.close()
        return nc, sch

    ph4 = ExitStack()
    Sp = ph4.enter_context
    wbf4 = [Sp(nc.sbuf_tensor(f"wbf4_{i}", [P, 8192], BF16)) for i in range(2)]
    gpt = Sp(nc.sbuf_tensor("gpt", [P, 512], BF16))
    epbt2 = Sp(nc.sbuf_tensor("epbt2", [64, 2048], BF16))
    epbt = epbt2[0:32, :]
    qabs = [Sp(nc.sbuf_tensor(f"qabs{i}", [P, 512], BF16)) for i in range(2)]
    scal_t = Sp(nc.sbuf_tensor("scal_t", [32, 512], BF16))
    rs_t = Sp(nc.sbuf_tensor("rs_t", [32, 512], F32))
    qp_t = Sp(nc.sbuf_tensor("qp_t", [P, 512], BF16))
    rs_hl = Sp(nc.sbuf_tensor("rs_hl", [64, 512], BF16))
    rs_lo = Sp(nc.sbuf_tensor("rs_lo", [32, 512], BF16))
    zp_t = Sp(nc.sbuf_tensor("zp_t", [P, 512], BF16))
    zn_t = Sp(nc.sbuf_tensor("zn_t", [P, 512], BF16))
    u_sb = Sp(nc.sbuf_tensor("u_sb", [P, 4 * 1024], BF16))
    u2_sb = Sp(nc.sbuf_tensor("u2_sb", [P, 1024], BF16))
    w2ob = Sp(nc.sbuf_tensor("w2ob", [P, 16 * 512], BF16))
    rstd_r = Sp(nc.sbuf_tensor("rstd_r", [1, NTOK], F32))
    ps_g4 = Sp(nc.psum_tensor("ps_g4", [32, 512], F32))
    ps_rs4 = Sp(nc.psum_tensor("ps_rs4", [P, 1024], F32))
    ps_sc4 = Sp(nc.psum_tensor("ps_sc4", [P, 1024], F32))
    ps_h = Sp(nc.psum_tensor("ps_h", [P, 1024], F32))
    ps_ms = Sp(nc.psum_tensor("ps_ms", [P, 512], F32))
    ms_acc = Sp(nc.sbuf_tensor("ms_acc", [1, NTOK], F32))
    st("sync", [], ["gpt"],
       lambda e: e.dma_start(out=gpt[:], in_=gp_d), dma="tab_gpt")
    st("sync", [], ["epbt", "epbt2"],
       lambda e: e.dma_start(out=epbt2[:], in_=epb2_d), dma="tab_epbt2")

    def quant_batch(src_r, colbase, chunk0, dst, out_view_fn, bufkey,
                    after_apply=None, width=512, hook=None, hook2=None):
        for cp in range(4):
            rns = sorted({out_view_fn(chunk0 + 4 * cp + j)[1] for j in range(4)})
            st("gpsimd", [], rns,
               (lambda e, cp=cp, chunk0=chunk0, colbase=colbase, src_r=src_r,
                        width=width, dst=dst:
                e.dma_start(
                    out=dst[:, 4 * cp * width:(4 * cp + 4) * width].rearrange(
                        "p (c q) -> p c q", c=4),
                    in_=src_r[chunk0 + 4 * cp:chunk0 + 4 * cp + 4, :,
                              colbase:colbase + width]
                        .rearrange("c p q -> p c q"))),
               dma=f"wq{bufkey}_{cp}")
        for c in range(16):
            cg = chunk0 + c
            ov, orn = out_view_fn(cg)
            if hook is not None and c % 2 == 0:
                hook()
            st("scalar", [orn], [f"qabs{c % 2}"],
               (lambda e, ov=ov, c=c, width=width:
                e.activation(out=qabs[c % 2][:, :width], in_=ov, func=AF.Abs)))
            st("tensor", [f"qabs{c % 2}", "gpt"], ["ps_g4"],
               (lambda e, c=c, width=width: e.matmul(
                   out=ps_g4[:, :width], lhsT=gpt[:, 32 * c:32 * (c + 1)],
                   rhs=qabs[c % 2][:, :width],
                   start=(c == 0), stop=(c == 15))))
            if hook2 is not None and c % 2 == 1:
                hook2()
        st("vector", ["ps_g4"], ["scal_t", "rs_t", "rs_hl", "rs_lo"],
           (lambda e, width=width: (
               e.tensor_scalar(out=scal_t[:, :width], in0=ps_g4[:, :width],
                               scalar1=1.0 / 64, scalar2=1e-8,
                               op0=ALU.mult, op1=ALU.max),
               e.reciprocal(out=rs_t[:, :width], in_=scal_t[:, :width]),
               e.tensor_copy(out=rs_hl[0:32, :width], in_=rs_t[:, :width]),
               e.tensor_sub(out=rs_lo[:, :width], in0=rs_t[:, :width],
                            in1=rs_hl[0:32, :width]),
               e.tensor_copy(out=rs_hl[32:64, :width],
                             in_=rs_lo[:, :width]))[-1]))
        for c in range(16):
            cg = chunk0 + c
            ov, orn = out_view_fn(cg)
            pp = c % 2
            o0 = 512 * pp

            def mm_exp(e, c=c, width=width, o0=o0):
                e.matmul(out=ps_rs4[:, o0:o0 + width],
                         lhsT=epbt2[:, P * c:P * (c + 1)],
                         rhs=rs_hl[:, :width], start=True, stop=True)
                return e.matmul(out=ps_sc4[:, o0:o0 + width],
                                lhsT=epbt2[0:32, P * c:P * (c + 1)],
                                rhs=scal_t[:, :width], start=True, stop=True)
            st("tensor", ["rs_hl", "scal_t", "epbt", "epbt2"],
               [f"ps_rs4{pp}", f"ps_sc4{pp}"], mm_exp)

            st("vector", [orn, f"ps_rs4{pp}"], [f"qmul{pp}"],
               (lambda e, ov=ov, width=width, o0=o0: e.tensor_mul(
                   out=qp_t[:, :width], in0=ov,
                   in1=ps_rs4[:, o0:o0 + width])))

            def do_cmp(e, width=width):
                e.tensor_scalar(out=zp_t[:, :width],
                                in0=qp_t[:, :width],
                                scalar1=0.5, scalar2=None, op0=ALU.is_gt)
                i = e.tensor_scalar(out=zn_t[:, :width],
                                    in0=qp_t[:, :width],
                                    scalar1=-0.5, scalar2=None, op0=ALU.is_lt)
                return i
            st("vector", [f"qmul{pp}"], [f"qzz{pp}"], do_cmp)
            st("vector", [f"qzz{pp}", f"ps_sc4{pp}"], [orn],
               (lambda e, ov=ov, width=width, pp=pp, o0=o0: (
                   e.tensor_sub(out=zp_t[:, :width],
                                in0=zp_t[:, :width],
                                in1=zn_t[:, :width]),
                   e.tensor_mul(out=ov, in0=zp_t[:, :width],
                                in1=ps_sc4[:, o0:o0 + width]))[-1]))
            if after_apply is not None:
                after_apply(cg)

    def w2_view(cg):
        cc = cg % 16
        return w2ob[:, cc * 512:(cc + 1) * 512], f"w2o{cc}"

    def fc_pieces(s):
        half = s % 2
        msp = []
        thunks = []
        for m in range(4):
            for th in range(2):
                def piece_mm(s=s, m=m, th=th, half=half, msp=msp):
                    pb = (2 * m + th) % 2

                    def mm_fc(e, m=m, th=th, pb=pb, half=half):
                        for c in range(NCH):
                            i = e.matmul(
                                out=ps_h[:, pb * 512:(pb + 1) * 512],
                                lhsT=wbf4[half][:, c * 512 + m * P:
                                                c * 512 + (m + 1) * P],
                                rhs=xn[:, c * TOKW + HALO + th * 512:
                                       c * TOKW + HALO + (th + 1) * 512],
                                start=(c == 0), stop=(c == NCH - 1))
                        return i
                    st("tensor",
                       [f"w1c{half}_{cc}" for cc in range(16)] + xnr,
                       [f"ps_h{pb}"], mm_fc)
                    for f in msp:
                        f()
                    msp.clear()

                def piece(s=s, m=m, th=th, half=half, msp=msp):
                    pb = (2 * m + th) % 2
                    st("vector", [f"ps_h{pb}"], [f"u{m}"],
                       (lambda e, m=m, th=th, pb=pb: e.tensor_scalar(
                           out=u_sb[:, m * 1024 + th * 512:m * 1024 + (th + 1) * 512],
                           in0=ps_h[:, pb * 512:(pb + 1) * 512],
                           scalar1=0.0, scalar2=None, op0=ALU.max)))
                    st("scalar", [f"u{m}"], [f"u{m}"],
                       (lambda e, m=m, th=th: e.activation(
                           out=u_sb[:, m * 1024 + th * 512:m * 1024 + (th + 1) * 512],
                           in_=u_sb[:, m * 1024 + th * 512:m * 1024 + (th + 1) * 512],
                           func=AF.Square)))
                    st("scalar", [f"u{m}"], [f"u2{th}"],
                       (lambda e, m=m, th=th: e.activation(
                           out=u2_sb[:, th * 512:(th + 1) * 512],
                           in_=u_sb[:, m * 1024 + th * 512:m * 1024 + (th + 1) * 512],
                           func=AF.Square)))
                    first = (s == 0 and m == 0)
                    last = (s == W1_SLABS - 1 and m == 3)
                    msp.append(lambda th=th, first=first, last=last: st(
                        "tensor", [f"u2{th}", "onesb"], ["ps_ms"],
                        (lambda e, th=th, first=first, last=last: e.matmul(
                            out=ps_ms[64 * th:64 * th + 1, :],
                            lhsT=onesb[:],
                            rhs=u2_sb[:, th * 512:(th + 1) * 512],
                            start=first, stop=last))))
                    if th == 1:
                        hid0 = 512 * s + m * P
                        st("gpsimd", [f"u{m}"], [f"udram{s}_{m}"],
                           (lambda e, hid0=hid0, m=m: e.dma_start(
                               out=u_scr[hid0:hid0 + P, :],
                               in_=u_sb[:, m * 1024:(m + 1) * 1024])),
                           dma=f"ust{m}")
                thunks.append((piece_mm, piece))

        def final(msp=msp):
            for f in msp:
                f()
            msp.clear()
        return thunks, final

    fcq = []
    restq = []
    fin_prev = None

    def hk():
        if fcq:
            mm_t, rest_t = fcq.pop(0)
            mm_t()
            restq.append(rest_t)

    def hk2():
        if restq:
            restq.pop(0)()

    for s in range(W1_SLABS):
        half = s % 2

        def w1_view(c, half=half):
            return wbf4[half][:, c * 512:(c + 1) * 512], f"w1c{half}_{c}"
        quant_batch(w1_r, 512 * s, 0, wbf4[half], w1_view, f"w1{half}",
                    hook=hk, hook2=hk2)
        while fcq or restq:
            hk()
            hk2()
        if fin_prev is not None:
            fin_prev()
        ws = s // 4
        wb_ = s % 4

        def w2_store(cg, ws=ws):
            cc = cg % 16
            st("gpsimd", [f"w2o{cc}"], [f"w2qdram{ws}_{cg}"],
               (lambda e, ws=ws, cg=cg, cc=cc: e.dma_start(
                   out=w2q_scr[P * cg:P * (cg + 1), 512 * ws:512 * (ws + 1)],
                   in_=w2ob[:, cc * 512:(cc + 1) * 512])),
               dma=f"w2st{cc % 4}")
        quant_batch(w2_r, 512 * ws, 16 * wb_, w2ob, w2_view, "w2",
                    after_apply=w2_store)
        fcq, fin_prev = fc_pieces(s)
    while fcq or restq:
        hk()
        hk2()
    if fin_prev is not None:
        fin_prev()
    st("vector", ["ps_ms"], ["ms_acc"],
       lambda e: (e.tensor_copy(out=ms_acc[:, 0:512], in_=ps_ms[0:1, :]),
                  e.tensor_copy(out=ms_acc[:, 512:1024], in_=ps_ms[64:65, :]))[-1])

    st("scalar", ["ms_acc", "epsrms"], ["rstd_r"],
       lambda e: (e.activation(out=rstd_r[:], in_=ms_acc[:], func=AF.Ln,
                               bias=epsrms[:], scale=1.0 / 8192),
                  e.activation(out=rstd_r[:], in_=rstd_r[:], func=AF.Exp,
                               scale=-0.5))[-1])

    def mm_rstd(e):
        e.matmul(out=ps_h[:, 0:512], lhsT=ones_row[:], rhs=rstd_r[:, 0:512],
                 start=True, stop=True)
        return e.matmul(out=ps_h[:, 512:1024], lhsT=ones_row[:],
                        rhs=rstd_r[:, 512:1024], start=True, stop=True)
    st("tensor", ["rstd_r", "ones_row"], ["ps_h0", "ps_h1"], mm_rstd)
    st("vector", ["ps_h0", "ps_h1"], ["rstd128"],
       lambda e: e.tensor_copy(out=rstd128[:], in_=ps_h[:, 0:NTOK]))
    if DEBUG:
        st("gpsimd", ["rstd128"], [],
           lambda e: e.dma_start(out=dbg["rstd"], in_=rstd128[:]), dma="dbg")

    ph4.close()
    sch.fence()

    if PHASES < 5:
        st("gpsimd", [f"acc{m}" for m in range(16)] + ["rstd128"], ["odram"],
           lambda e: e.dma_start(
               out=o_fm.rearrange("(c p) q -> c p q", p=P).rearrange("c p q -> p c q"),
               in_=acc[:].rearrange("p (c q) -> p c q", c=NCH)),
           dma="ost")
        fv = sch.dma["ost"][1]
        sch.prog["gpsimd"].append(
            lambda e, fv=fv: e.wait_ge(sch.dma["ost"][0], fv))
        sch.emit()
        root.close()
        return nc, sch

    ph5 = ExitStack()
    Sp = ph5.enter_context
    w2b = [Sp(nc.sbuf_tensor(f"w2b{i}", [P, 4 * 512], BF16)) for i in range(2)]
    u_ld = [Sp(nc.sbuf_tensor(f"u_ld{i}", [P, 4 * 1024], BF16)) for i in range(2)]
    tmul = Sp(nc.sbuf_tensor("tmul", [P, 1024], F32))
    ps_p = Sp(nc.psum_tensor("ps_p", [P, 4 * 1024], F32))
    w2q_r = w2q_scr.rearrange("(c p) q -> c p q", p=P)
    u_r = u_scr.rearrange("(c p) q -> c p q", p=P)

    uall = [f"udram{s}_{m}" for s in range(W1_SLABS) for m in range(4)]
    w2all = [f"w2qdram{s}_{c}" for s in range(W2_SLABS) for c in range(64)]

    for dq in range(4):
        for kg in range(16):
            b = kg % 2
            first = (dq == 0 and kg < 2)
            st("sync", w2all if first else [], [f"w2b{b}"],
               (lambda e, dq=dq, kg=kg, b=b: e.dma_start(
                   out=w2b[b][:].rearrange("p (c q) -> p c q", c=4),
                   in_=w2q_r[4 * kg:4 * kg + 4, :, 512 * dq:512 * (dq + 1)]
                       .rearrange("c p q -> p c q"))),
               dma=f"w2b{b}")
            st("sync", uall if first else [], [f"u_ld{b}"],
               (lambda e, kg=kg, b=b: e.dma_start(
                   out=u_ld[b][:].rearrange("p (c q) -> p c q", c=4),
                   in_=u_r[4 * kg:4 * kg + 4, :, :]
                       .rearrange("c p q -> p c q"))),
               dma=f"uld{b}")

            def mm_p(e, kg=kg, b=b):
                for cc in range(4):
                    for dm in range(4):
                        for th in range(2):
                            i = e.matmul(
                                out=ps_p[:, dm * 1024 + th * 512:
                                         dm * 1024 + (th + 1) * 512],
                                lhsT=w2b[b][:, cc * 512 + dm * P:cc * 512 + (dm + 1) * P],
                                rhs=u_ld[b][:, cc * 1024 + th * 512:
                                            cc * 1024 + (th + 1) * 512],
                                start=(kg == 0 and cc == 0),
                                stop=(kg == 15 and cc == 3))
                return i
            st("tensor", [f"w2b{b}", f"u_ld{b}"],
               [f"ps_p{d}" for d in range(4)], mm_p)
        for dm in range(4):
            adm = 4 * dq + dm
            st("vector", [f"ps_p{dm}", "rstd128", f"acc{adm}"],
               [f"acc{adm}", "tmul"],
               (lambda e, dm=dm, adm=adm: (
                   e.tensor_mul(out=tmul[:], in0=ps_p[:, dm * 1024:(dm + 1) * 1024],
                                in1=rstd128[:]),
                   e.tensor_add(out=acc[:, adm * NTOK:(adm + 1) * NTOK],
                                in0=acc[:, adm * NTOK:(adm + 1) * NTOK],
                                in1=tmul[:]))[-1]))
    ph5.close()

    st("gpsimd", [f"acc{m}" for m in range(16)], ["odram"],
       lambda e: e.dma_start(
           out=o_fm.rearrange("(c p) q -> c p q", p=P).rearrange("c p q -> p c q"),
           in_=acc[:].rearrange("p (c q) -> p c q", c=NCH)),
       dma="ost")
    fv = sch.dma["ost"][1]
    sch.prog["gpsimd"].append(
        lambda e, fv=fv: e.wait_ge(sch.dma["ost"][0], fv))

    sch.emit()
    root.close()
    return nc, sch


def kernel(L0_attn_x, L0_attn_qkv_w, L0_attn_qkv_b, L0_attn_out_w,
           L0_attn_out_b, L0_attn_sinks, L0_ffn_W_fc, L0_ffn_W_proj):
    sh, cores = _host_prep(L0_attn_x, L0_attn_qkv_w, L0_attn_qkv_b,
                           L0_attn_out_w, L0_attn_out_b, L0_attn_sinks,
                           L0_ffn_W_fc, L0_ffn_W_proj)
    nc, _ = build_nc()
    in_maps = []
    for cid in range(8):
        m = dict(sh)
        m.update(cores[cid])
        in_maps.append(m)
    res = run_bass_kernel_spmd(nc, in_maps, core_ids=list(range(8)))
    out = np.empty((2, 4096, DMODEL), np.float32)
    for cid in range(8):
        b, j = cid // 4, cid % 4
        out[b, j * NTOK:(j + 1) * NTOK] = np.asarray(res.results[cid]["o_fm"]).T
    return out
